# version 22
# baseline (speedup 1.0000x reference)
import math
import numpy as np
import ml_dtypes
import concourse.bass as bass
import concourse.mybir as mybir
from concourse.bass_utils import run_bass_kernel_spmd
from contextlib import ExitStack

F32 = mybir.dt.float32
BF16 = mybir.dt.bfloat16
AF = mybir.ActivationFunctionType
ALU = mybir.AluOpType

T = 2048
D = 2048
H = 16
TH = 1024
IC = 16
EPS_LN = 1e-5
EPS_RMS = float(np.float32(2.0 ** -23))
INV_D = 1.0 / 2048.0
INV_DH = 1.0 / 128.0
GG_SCALE = 1.0 / math.sqrt(128.0)
C_SHIFT = -12.0
ENGS = ("sync", "gpsimd", "tensor", "vector", "scalar")


class SemT:
    def __init__(self, handle):
        self.h = handle
        self.c = 0


class _DryInst:
    def then_inc(self, sem, n=1):
        return self


_DRY = _DryInst()


class DryEng:
    def __getattr__(self, name):
        def _f(*a, **k):
            return _DRY
        return _f


class _RealInst:
    __slots__ = ("i",)

    def __init__(self, inst):
        self.i = inst

    def then_inc(self, sem, n=1):
        self.i.then_inc(sem.h, n)
        return self


class RealEng:
    def __init__(self, eng):
        object.__setattr__(self, "_e", eng)

    def __getattr__(self, name):
        meth = getattr(self._e, name)

        def _f(*a, **k):
            return _RealInst(meth(*a, **k))
        return _f


class Prog:
    def __init__(self, nc):
        self.nc = nc
        self.dry = True
        self.ev = {}

    def mark(self, name, sem):
        if self.dry:
            assert name not in self.ev, name
            self.ev[name] = sem.c

    def inc(self, inst, sem, n=1, mark=None):
        if self.dry:
            sem.c += n
            if mark is not None:
                self.mark(mark, sem)
        else:
            inst.then_inc(sem, n)

    def w(self, eng, sem, name):
        if not self.dry:
            eng._e.wait_ge(sem.h, self.ev[name])

    def wv(self, eng, sem, value):
        if not self.dry:
            eng._e.wait_ge(sem.h, value)

    def run_phase(self, bodies):
        self.dry = True
        for k in ENGS:
            if k in bodies:
                bodies[k](DryEng())
        self.dry = False
        with self.nc.Block() as blk:
            for k in ENGS:
                if k in bodies:
                    fn = bodies[k]
                    getattr(blk, k)(lambda e, fn=fn: fn(RealEng(e)))


def build_nc(dbg=False):
    _ING = "ExternalOutput" if dbg else "Internal"
    nc = bass.Bass("TRN2", target_bir_lowering=False, debug=False,
                   detect_race_conditions=False)

    xT = nc.dram_tensor("xT", [D, T], F32, kind="ExternalInput").ap()
    xhd = nc.dram_tensor("xhd", [TH, D], F32, kind="ExternalInput").ap()
    WT = nc.dram_tensor("WTm", [D, D], F32, kind="ExternalInput").ap()
    Wn = nc.dram_tensor("Wn", [D, D], F32, kind="ExternalInput").ap()
    wgtT = nc.dram_tensor("wgtT", [D, D], F32, kind="ExternalInput").ap()
    biastd = nc.dram_tensor("biastd", [128, 16], F32, kind="ExternalInput").ap()
    g1d = nc.dram_tensor("g1d", [128, 1], F32, kind="ExternalInput").ap()
    g2d = nc.dram_tensor("g2d", [128, 1], F32, kind="ExternalInput").ap()
    g3bcd = nc.dram_tensor("g3bcd", [128, 128], F32, kind="ExternalInput").ap()
    identd = nc.dram_tensor("identd", [128, 128], F32, kind="ExternalInput").ap()
    seld = nc.dram_tensor("seld", [16, 2048], F32, kind="ExternalInput").ap()
    aonesd = nc.dram_tensor("aonesd", [128, 128], F32, kind="ExternalInput").ap()
    negcbfd = nc.dram_tensor("negcbfd", [128, 1], BF16, kind="ExternalInput").ap()
    onescd = nc.dram_tensor("onescd", [128, 1], F32, kind="ExternalInput").ap()
    cshiftd = nc.dram_tensor("cshiftd", [128, 1], F32, kind="ExternalInput").ap()

    out = nc.dram_tensor("out", [TH, D], F32, kind="ExternalOutput").ap()

    wsc = nc.dram_tensor("wsc", [H, 128, T], F32, kind=_ING).ap()
    wscT = nc.dram_tensor("wscT", [H, 128, T], F32, kind=_ING).ap()
    yhsc = nc.dram_tensor("yhsc", [H, 128, TH], BF16, kind=_ING).ap()
    ylsc = nc.dram_tensor("ylsc", [H, 128, TH], BF16, kind=_ING).ap()
    resF = nc.dram_tensor("resF", [D, TH], F32, kind=_ING).ap()
    if dbg:
        dbg_rsq = nc.dram_tensor("dbg_rsq", [128, 16], F32, kind="ExternalOutput").ap()
        dbg_ms = nc.dram_tensor("dbg_ms", [128, 16], F32, kind="ExternalOutput").ap()
        dbg_rbc = nc.dram_tensor("dbg_rbc", [128, T], F32, kind="ExternalOutput").ap()
        dbg_mrbc = nc.dram_tensor("dbg_mrbc", [128, T], F32, kind="ExternalOutput").ap()
        dbg_xlo = nc.dram_tensor("dbg_xlo", [128, IC, T], BF16, kind="ExternalOutput").ap()
        dbg_pw = nc.dram_tensor("dbg_pw", [128, T], F32, kind="ExternalOutput").ap()
        dbg_csn = nc.dram_tensor("dbg_csn", [128, 16], F32, kind="ExternalOutput").ap()
        dbg_msr = nc.dram_tensor("dbg_msr", [128, 16], F32, kind="ExternalOutput").ap()
        dbg_mst = nc.dram_tensor("dbg_mst", [128, 16], F32, kind="ExternalOutput").ap()
        dbg_sq0 = nc.dram_tensor("dbg_sq0", [128, 16], F32, kind="ExternalOutput").ap()
        dbg_z = nc.dram_tensor("dbg_z", [128, 16], F32, kind="ExternalOutput").ap()
        dbg_zi = nc.dram_tensor("dbg_zi", [128, 16], F32, kind="ExternalOutput").ap()
        dbg_ehi = nc.dram_tensor("dbg_ehi", [128, 16, TH], BF16, kind="ExternalOutput").ap()
        dbg_elo = nc.dram_tensor("dbg_elo", [128, 16, TH], BF16, kind="ExternalOutput").ap()
        dbg_qf = nc.dram_tensor("dbg_qf", [128, T], F32, kind="ExternalOutput").ap()
        dbg_khi = nc.dram_tensor("dbg_khi", [128, T], BF16, kind="ExternalOutput").ap()
        dbg_wtok = nc.dram_tensor("dbg_wtok", [128, 16, 128], F32, kind="ExternalOutput").ap()
        dbg_vf = nc.dram_tensor("dbg_vf", [128, 16, 128], F32, kind="ExternalOutput").ap()
        dbg_zbc = nc.dram_tensor("dbg_zbc", [128, TH], F32, kind="ExternalOutput").ap()
        dbg_att = nc.dram_tensor("dbg_att", [128, TH], F32, kind="ExternalOutput").ap()
        dbg_resb = nc.dram_tensor("dbg_resb", [128, TH], F32, kind="ExternalOutput").ap()
        dbg_xhb2 = nc.dram_tensor("dbg_xhb2", [128, TH], F32, kind="ExternalOutput").ap()

    xTv = xT.rearrange("(ic p) t -> p ic t", p=128)
    WTv = WT.rearrange("(ic p) o -> p ic o", p=128)
    wgtTv = wgtT.rearrange("(ic p) o -> p ic o", p=128)
    Wnv = Wn.rearrange("(fc p) c -> p fc c", p=128)
    xhv = xhd.rearrange("(tc p) c -> p tc c", p=128)
    resFv = resF.rearrange("(oc p) t -> p oc t", p=128)
    outv = out.rearrange("(tc p) c -> p tc c", p=128)

    top = ExitStack()
    _sems = ("PEa", "PEb", "VEa", "VEb", "ACa", "ACb",
             "CONST", "XF0", "XF1", "XH0", "XH1", "XH2", "XH3",
             "WTF0", "WTF1", "WTH0", "WTH1", "WST",
             "WGF0", "WGF1", "RST0", "RST1",
             "WLD0", "WLD1", "YST",
             "P3Y", "WF0", "WF1", "RX0", "RX1", "OST0", "OST1", "DBG", "WTT", "WTK0", "WTK1",
             *[f"WHI{i}" for i in range(16)])
    S = {n: SemT(top.enter_context(nc.semaphore("s" + n))) for n in _sems}
    sPEa, sPEb, sVEa, sVEb, sACa, sACb = (S[n] for n in
                                          ("PEa", "PEb", "VEa", "VEb", "ACa", "ACb"))
    sCONST = S["CONST"]
    sXF = [S["XF0"], S["XF1"]]
    sXH = [S["XH0"], S["XH1"], S["XH2"], S["XH3"]]
    sWTF = [S["WTF0"], S["WTF1"]]
    sWTH = [S["WTH0"], S["WTH1"]]
    sWST = S["WST"]
    sWGF = [S["WGF0"], S["WGF1"]]
    sRST = [S["RST0"], S["RST1"]]
    sWLD = [S["WLD0"], S["WLD1"]]
    sYST = S["YST"]
    sP3Y = S["P3Y"]
    sWF = [S["WF0"], S["WF1"]]
    sWHIF = [S[f"WHI{i}"] for i in range(16)]
    sRX = [S["RX0"], S["RX1"]]
    sOST = [S["OST0"], S["OST1"]]
    sDBG = S["DBG"]
    sWTT = S["WTT"]
    sWTK = [S["WTK0"], S["WTK1"]]

    ident = top.enter_context(nc.sbuf_tensor("ident", [128, 128], F32))
    aones = top.enter_context(nc.sbuf_tensor("aones", [128, 128], F32))
    negcbf = top.enter_context(nc.sbuf_tensor("negcbf", [128, 1], BF16))
    onesc = top.enter_context(nc.sbuf_tensor("onesc", [128, 1], F32))
    cshift = top.enter_context(nc.sbuf_tensor("cshift", [128, 1], F32))
    g3bc = top.enter_context(nc.sbuf_tensor("g3bc", [128, 128], F32))
    gg = top.enter_context(nc.sbuf_tensor("gg", [128, 1], F32))
    g1s = top.enter_context(nc.sbuf_tensor("g1s", [128, 1], F32))
    g2s = top.enter_context(nc.sbuf_tensor("g2s", [128, 1], F32))
    biast = top.enter_context(nc.sbuf_tensor("biast", [128, 16], F32))
    vdum = top.enter_context(nc.sbuf_tensor("vdum", [128, 1], F32))
    adum = top.enter_context(nc.sbuf_tensor("adum", [128, 1], F32))

    P = Prog(nc)
    mark, inc, w = P.mark, P.inc, P.w

    def vguard(ve):
        ve.tensor_copy(vdum[:], vdum[:])

    def aguard(sc):
        sc.activation(adum[:], cshift[:], AF.Copy)

    mid = ExitStack()
    xhi = mid.enter_context(nc.sbuf_tensor("xhi", [128, IC, T], BF16))
    xlo = mid.enter_context(nc.sbuf_tensor("xlo", [128, IC, T], BF16))
    rbc = mid.enter_context(nc.sbuf_tensor("rbc", [128, T], F32))
    mrbc = mid.enter_context(nc.sbuf_tensor("mrbc", [128, T], F32))

    with ExitStack() as ph:
        xf = ph.enter_context(nc.sbuf_tensor("xf", [128, 2, 4, 512], F32))
        sq = ph.enter_context(nc.sbuf_tensor("sq", [128, 2, 512], F32))
        mubc = ph.enter_context(nc.sbuf_tensor("mubc", [128, T], F32))
        vtmp = ph.enter_context(nc.sbuf_tensor("vtmp", [128, T], F32))
        vare = ph.enter_context(nc.sbuf_tensor("vare", [128, T], F32))
        pmu = ph.enter_context(nc.psum_tensor("pmu", [128, 512], F32))
        psq = ph.enter_context(nc.psum_tensor("psq", [128, 512], F32))

        def p0_sync(sy):
            for dst, src in [(ident, identd), (aones, aonesd),
                             (negcbf, negcbfd), (onesc, onescd),
                             (cshift, cshiftd), (g3bc, g3bcd), (g1s, g1d),
                             (g2s, g2d), (biast, biastd)]:
                inc(sy.dma_start(out=dst[:], in_=src), sCONST, 16)
            mark("p0.consts", sCONST)
            for tg in range(4):
                for icg in range(4):
                    kk = tg * 4 + icg
                    if kk >= 2:
                        w(sy, sVEa, f"p0.xf_free.{kk - 2}")
                        w(sy, sPEa, f"p0.sqmm.{4 * (kk - 2) + 3}")
                        w(sy, sACa, f"p0.sq.{4 * (kk - 2) + 3}")
                    inc(sy.dma_start(out=xf[:, kk % 2],
                                     in_=xTv[:, icg * 4:(icg + 1) * 4,
                                             tg * 512:(tg + 1) * 512]),
                        sXF[kk % 2], 16, mark=f"p0.xf.{kk}")
            mark("p0.xff0", sXF[0])
            mark("p0.xff1", sXF[1])

        def p0_gp(g):
            for tg in range(4):
                inc(g.dma_start(out=xhi[:, :, tg * 512:(tg + 1) * 512],
                                in_=xTv[:, :, tg * 512:(tg + 1) * 512]),
                    sXH[tg], 16, mark=f"p0.xhi.{tg}")

        def p0_sc(sc):
            w(sc, sCONST, "p0.consts")
            k = 0
            for tg in range(4):
                for icg in range(4):
                    kk = tg * 4 + icg
                    w(sc, sXF[kk % 2], f"p0.xf.{kk}")
                    for icx in range(4):
                        if k >= 2:
                            w(sc, sPEa, f"p0.sqmm.{k - 2}")
                        inc(sc.activation(sq[:, k % 2], xf[:, kk % 2, icx, :],
                                          AF.Square),
                            sACa, 1, mark=f"p0.sq.{k}")
                        k += 1
            w(sc, sVEa, "p0.vare")
            sc.activation(vtmp[:], vare[:], AF.Ln)
            inc(sc.activation(rbc[:], vtmp[:], AF.Exp, scale=-0.5), sACa, 1,
                mark="p0.sqrt")

        def p0_te(te):
            k = 0
            for tg in range(4):
                if tg > 0:
                    w(te, sVEa, f"p0.statev.{tg - 1}")
                for icg in range(4):
                    kk = tg * 4 + icg
                    w(te, sXF[kk % 2], f"p0.xf.{kk}")
                    for icx in range(4):
                        te.matmul(pmu[:, :], aones[:], xf[:, kk % 2, icx, :],
                                  start=(icg == 0 and icx == 0),
                                  stop=(icg == 3 and icx == 3))
                    for icx in range(4):
                        w(te, sACa, f"p0.sq.{k}")
                        mm = te.matmul(psq[:, :], aones[:], sq[:, k % 2],
                                       start=(icg == 0 and icx == 0),
                                       stop=(icg == 3 and icx == 3))
                        inc(mm, sPEa, 1, mark=f"p0.sqmm.{k}")
                        k += 1
                mark(f"p0.stats.{tg}", sPEa)

        def p0_ve(ve):
            ve.memset(vdum[:], 0.0)
            w(ve, sCONST, "p0.consts")
            inc(ve.tensor_mul(gg[:], g1s[:], g2s[:]), sVEa, 1, mark="p0.gg1")
            w(ve, sVEa, "p0.gg1")
            ve.tensor_scalar_mul(gg[:], gg[:], GG_SCALE)
            for tg in range(4):
                w(ve, sXH[tg], f"p0.xhi.{tg}")
                for icg in range(4):
                    kk = tg * 4 + icg
                    w(ve, sXF[kk % 2], f"p0.xf.{kk}")
                    i1 = ve.tensor_tensor(
                        xlo[:, icg * 4:(icg + 1) * 4, tg * 512:(tg + 1) * 512],
                        xf[:, kk % 2],
                        xhi[:, icg * 4:(icg + 1) * 4, tg * 512:(tg + 1) * 512],
                        ALU.subtract)
                    inc(i1, sVEa, 1, mark=f"p0.xf_free.{kk}")
                w(ve, sPEa, f"p0.stats.{tg}")
                vguard(ve)
                ve.tensor_scalar_mul(mubc[:, tg * 512:(tg + 1) * 512],
                                     pmu[:, :], INV_D)
                i2 = ve.tensor_scalar_mul(vare[:, tg * 512:(tg + 1) * 512],
                                          psq[:, :], INV_D)
                inc(i2, sVEa, 1, mark=f"p0.statev.{tg}")
            ve.tensor_mul(vtmp[:], mubc[:], mubc[:])
            ve.tensor_sub(vare[:], vare[:], vtmp[:])
            inc(ve.tensor_scalar_add(vare[:], vare[:], EPS_LN), sVEa, 1,
                mark="p0.vare")
            w(ve, sACa, "p0.sqrt")
            for it in range(1):
                ve.tensor_mul(vtmp[:], rbc[:], rbc[:])
                inc(ve.tensor_mul(vtmp[:], vtmp[:], vare[:]), sVEa, 1,
                    mark=f"p0.nt{it}")
                w(ve, sVEa, f"p0.nt{it}")
                inc(ve.tensor_scalar(vtmp[:], vtmp[:], -0.5, 1.5, ALU.mult,
                                     ALU.add), sVEa, 1, mark=f"p0.nt{it}b")
                w(ve, sVEa, f"p0.nt{it}b")
                ve.tensor_mul(rbc[:], rbc[:], vtmp[:])
            inc(ve.tensor_mul(mrbc[:], mubc[:], rbc[:]), sVEb, 1,
                mark="p0.done")

        P.run_phase({"sync": p0_sync, "gpsimd": p0_gp, "scalar": p0_sc,
                     "tensor": p0_te, "vector": p0_ve})

    with ExitStack() as ph:
        wtf = ph.enter_context(nc.sbuf_tensor("wtf", [128, 2, IC, 128], F32))
        wthi = ph.enter_context(nc.sbuf_tensor("wthi", [128, 2, IC, 128], BF16))
        wtlo = ph.enter_context(nc.sbuf_tensor("wtlo", [128, 2, IC, 128], BF16))
        w_h = ph.enter_context(nc.sbuf_tensor("w_h", [128, 2, T], F32))
        wts = ph.enter_context(nc.sbuf_tensor("wts", [128, 16, 128], F32))
        csn = ph.enter_context(nc.sbuf_tensor("csn", [128, 1], F32))
        pw = ph.enter_context(nc.psum_tensor("pw", [128, T], F32))
        ptw0 = ph.enter_context(nc.psum_tensor("ptw0", [128, 512], F32))
        ptw1 = ph.enter_context(nc.psum_tensor("ptw1", [128, 512], F32))
        pcs = ph.enter_context(nc.psum_tensor("pcs", [128, 512], F32))

        def p1_sync(sy):
            def wstore(h):
                w(sy, sVEa, f"p1.wh.{h}")
                inc(sy.dma_start(out=wsc[h], in_=w_h[:, h % 2, :]), sWST, 16,
                    mark=f"p1.wst.{h}")

            def wtstore(h):
                w(sy, sVEa, f"p1.wtse.{h}")
                inc(sy.dma_start(out=wscT[h], in_=wts[:, :, :]), sWTT, 16,
                    mark=f"p1.wtt.{h}")

            for h in range(H):
                s = h % 2
                if h == 0:
                    w(sy, sVEb, "p0.done")
                    w(sy, sXF[0], "p0.xff0")
                    w(sy, sXF[1], "p0.xff1")
                    if dbg:
                        inc(sy.dma_start(out=dbg_rbc, in_=rbc[:]), sDBG, 16)
                        inc(sy.dma_start(out=dbg_mrbc, in_=mrbc[:]), sDBG, 16)
                        inc(sy.dma_start(out=dbg_xlo, in_=xlo[:, :, :]), sDBG, 16)
                if h >= 2:
                    w(sy, sPEb, f"p1.proj.{h - 2}")
                inc(sy.dma_start(out=wtf[:, s], in_=WTv[:, :, h * 128:(h + 1) * 128]),
                    sWTF[s], 16, mark=f"p1.wtf.{h}")
                if h >= 1:
                    wstore(h - 1)
                if h >= 2:
                    wtstore(h - 2)
            wstore(H - 1)
            wtstore(H - 2)
            wtstore(H - 1)

        def p1_gp(g):
            for h in range(H):
                s = h % 2
                if h == 0:
                    w(g, sVEb, "p0.done")
                    w(g, sXF[0], "p0.xff0")
                    w(g, sXF[1], "p0.xff1")
                if h >= 2:
                    w(g, sPEb, f"p1.proj.{h - 2}")
                inc(g.dma_start(out=wthi[:, s], in_=WTv[:, :, h * 128:(h + 1) * 128]),
                    sWTH[s], 16, mark=f"p1.wthi.{h}")

        def p1_te(te):
            def trb(te, h):
                w(te, sVEa, f"p1.wh.{h}")
                ptw = [ptw0, ptw1]
                for tc in range(16):
                    if tc >= 2:
                        w(te, sVEa, f"p1.wtev.{h}.{tc - 2}")
                    elif h >= 1:
                        w(te, sVEa, f"p1.wtev.{h - 1}.{14 + tc}")
                    tt = te.transpose(ptw[tc % 2][:, 0:128],
                                      w_h[:, h % 2, tc * 128:(tc + 1) * 128],
                                      ident[:])
                    inc(tt, sPEa, 1, mark=f"p1.tr.{h}.{tc}")

            for h in range(H):
                s = h % 2
                w(te, sVEa, f"p1.wtlo.{h}")
                if h == 0:
                    w(te, sVEb, "p0.done")
                if h >= 1:
                    w(te, sVEa, f"p1.ep1.{h - 1}")
                mm = None
                for icx in range(IC):
                    for tg in range(4):
                        te.matmul(pw[:, tg * 512:(tg + 1) * 512],
                                  wthi[:, s][:, icx, :],
                                  xhi[:, icx, tg * 512:(tg + 1) * 512],
                                  start=(icx == 0), stop=False)
                    for tg in range(4):
                        te.matmul(pw[:, tg * 512:(tg + 1) * 512],
                                  wthi[:, s][:, icx, :],
                                  xlo[:, icx, tg * 512:(tg + 1) * 512],
                                  start=False, stop=False)
                    te.matmul(pcs[:, 0:1], wthi[:, s][:, icx, :], negcbf[:],
                              start=(icx == 0), stop=False)
                    for tg in range(4):
                        te.matmul(pw[:, tg * 512:(tg + 1) * 512],
                                  wtlo[:, s][:, icx, :],
                                  xhi[:, icx, tg * 512:(tg + 1) * 512],
                                  start=False, stop=(icx == IC - 1))
                    mm = te.matmul(pcs[:, 0:1], wtlo[:, s][:, icx, :], negcbf[:],
                                   start=False, stop=(icx == IC - 1))
                inc(mm, sPEb, 1, mark=f"p1.proj.{h}")
                if h >= 1:
                    trb(te, h - 1)
            trb(te, H - 1)

        def p1_ve(ve):
            def epi(h):
                w(ve, sPEb, f"p1.proj.{h}")
                if h >= 2:
                    w(ve, sWST, f"p1.wst.{h - 2}")
                    w(ve, sPEa, f"p1.tr.{h - 2}.15")
                vguard(ve)
                ve.tensor_copy(csn[:], pcs[:, 0:1])
                inc(ve.tensor_tensor(w_h[:, h % 2, :], pw[:, :], rbc[:],
                                     ALU.mult),
                    sVEa, 1, mark=f"p1.ep1.{h}")
                inc(ve.scalar_tensor_tensor(w_h[:, h % 2, :], mrbc[:], csn[:, 0:1],
                                            w_h[:, h % 2, :], ALU.mult, ALU.add),
                    sVEa, 1, mark=f"p1.wh.{h}")

            def trev(h):
                if h >= 1:
                    w(ve, sWTT, f"p1.wtt.{h - 1}")
                for tc in range(16):
                    w(ve, sPEa, f"p1.tr.{h}.{tc}")
                    vguard(ve)
                    ptw = [ptw0, ptw1]
                    i1 = ve.tensor_copy(wts[:, tc, :], ptw[tc % 2][:, 0:128])
                    inc(i1, sVEa, 1, mark=f"p1.wtev.{h}.{tc}")
                mark(f"p1.wtse.{h}", sVEa)

            for h in range(H):
                s2 = h % 2
                w(ve, sWTF[s2], f"p1.wtf.{h}")
                w(ve, sWTH[s2], f"p1.wthi.{h}")
                if h >= 2:
                    w(ve, sPEb, f"p1.proj.{h - 2}")
                inc(ve.tensor_tensor(wtlo[:, s2], wtf[:, s2], wthi[:, s2],
                                     ALU.subtract), sVEa, 1,
                    mark=f"p1.wtlo.{h}")
                if h >= 1:
                    epi(h - 1)
                if h >= 2:
                    trev(h - 2)
            epi(H - 1)
            trev(H - 2)
            trev(H - 1)

        P.run_phase({"sync": p1_sync, "gpsimd": p1_gp, "tensor": p1_te,
                     "vector": p1_ve})

    with ExitStack() as ph:
        wgf = ph.enter_context(nc.sbuf_tensor("wgf", [128, 2, IC, 128], F32))
        sgnc = ph.enter_context(nc.sbuf_tensor("sgnc", [128, 2, IC, 128], BF16))
        absc = ph.enter_context(nc.sbuf_tensor("absc", [128, 8, 128], F32))
        negcs = ph.enter_context(nc.sbuf_tensor("negcs", [128, 16], F32))
        ainv = ph.enter_context(nc.sbuf_tensor("ainv", [128, 16], F32))
        ta = ph.enter_context(nc.sbuf_tensor("ta", [128, TH], F32))
        tb = ph.enter_context(nc.sbuf_tensor("tb", [128, TH], F32))
        rT = ph.enter_context(nc.sbuf_tensor("rT", [128, 2, TH], F32))
        pS = ph.enter_context(nc.psum_tensor("pS", [128, 2, TH], F32))
        pal = ph.enter_context(nc.psum_tensor("pal", [128, 512], F32))
        pcb = ph.enter_context(nc.psum_tensor("pcb", [128, 512], F32))

        def p15_sync(sy):
            for oc in range(16):
                s = oc % 2
                if oc == 0:
                    w(sy, sVEa, "p1.wh.15")
                    w(sy, sWST, "p1.wst.15")
                    w(sy, sWTF[0], "p1.wtf.14")
                    w(sy, sWTF[1], "p1.wtf.15")
                    w(sy, sWTH[0], "p1.wthi.14")
                    w(sy, sWTH[1], "p1.wthi.15")
                if oc >= 2:
                    w(sy, sVEa, f"p15.sgn.{oc - 2}")
                    w(sy, sACa, f"p15.abs.{oc - 2}")
                inc(sy.dma_start(out=wgf[:, s],
                                 in_=wgtTv[:, :, oc * 128:(oc + 1) * 128]),
                    sWGF[s], 16, mark=f"p15.wgf.{oc}")
                if oc >= 2:
                    w(sy, sVEa, f"p15.rT.{oc - 2}")
                    inc(sy.dma_start(out=resFv[:, oc - 2, :], in_=rT[:, (oc - 2) % 2]),
                        sRST[oc % 2], 16, mark=f"p15.rst.{oc - 2}")
            for oc in (14, 15):
                w(sy, sVEa, f"p15.rT.{oc}")
                inc(sy.dma_start(out=resFv[:, oc, :], in_=rT[:, oc % 2]),
                    sRST[oc % 2], 16, mark=f"p15.rst.{oc}")

        def p15_sc(sc):
            k = 0
            for oc in range(16):
                s = oc % 2
                w(sc, sWGF[s], f"p15.wgf.{oc}")
                for icx in range(IC):
                    if k >= 8:
                        w(sc, sPEa, f"p15.amm.{k - 8}")
                    inc(sc.activation(absc[:, k % 8], wgf[:, s][:, icx, :], AF.Abs),
                        sACa, 1, mark=f"p15.absk.{k}")
                    k += 1
                mark(f"p15.abs.{oc}", sACa)

        def p15_ve(ve):
            for oc in range(16):
                s = oc % 2
                w(ve, sWGF[s], f"p15.wgf.{oc}")
                if oc >= 2:
                    w(ve, sPEb, f"p15.S.{oc - 2}")
                inc(ve.tensor_scalar(sgnc[:, s], wgf[:, s], 0.0, 2.0,
                                     ALU.is_ge, ALU.mult), sVEa, 1,
                    mark=f"p15.sgn0.{oc}")
                w(ve, sVEa, f"p15.sgn0.{oc}")
                inc(ve.tensor_scalar_add(sgnc[:, s], sgnc[:, s], -1.0), sVEa, 1,
                    mark=f"p15.sgn.{oc}")
                w(ve, sPEb, f"p15.acs.{oc}")
                vguard(ve)
                ve.tensor_scalar_mul(ainv[:, oc:oc + 1], pal[:, 0:1], INV_D)
                inc(ve.tensor_copy(negcs[:, oc:oc + 1], pcb[:, 0:1]), sVEa, 1,
                    mark=f"p15.ae.{oc}")
                w(ve, sPEb, f"p15.S.{oc}")
                if oc >= 2:
                    w(ve, sRST[oc % 2], f"p15.rst.{oc - 2}")
                vguard(ve)
                ve.tensor_tensor(ta[:], pS[:, s, :], rbc[:, 0:TH], ALU.mult)
                ve.scalar_tensor_tensor(tb[:], mrbc[:, 0:TH], negcs[:, oc:oc + 1],
                                        ta[:], ALU.mult, ALU.add)
                inc(ve.tensor_scalar(rT[:, s][:, :], tb[:], ainv[:, oc:oc + 1],
                                     biast[:, oc:oc + 1], ALU.mult, ALU.add),
                    sVEa, 1, mark=f"p15.rT.{oc}")

        def p15_te(te):
            for oc in range(16):
                s = oc % 2
                if oc >= 1:
                    w(te, sVEa, f"p15.ae.{oc - 1}")
                mm = None
                for icx in range(IC):
                    w(te, sACa, f"p15.absk.{oc * IC + icx}")
                    mm = te.matmul(pal[:, 0:1], absc[:, (oc * IC + icx) % 8],
                                   onesc[:], start=(icx == 0),
                                   stop=(icx == IC - 1))
                    inc(mm, sPEa, 1, mark=f"p15.amm.{oc * IC + icx}")
                w(te, sVEa, f"p15.sgn.{oc}")
                for icx in range(IC):
                    mm = te.matmul(pcb[:, 0:1], sgnc[:, s][:, icx, :], negcbf[:],
                                   start=(icx == 0), stop=(icx == IC - 1))
                inc(mm, sPEb, 1, mark=f"p15.acs.{oc}")
                if oc >= 2:
                    w(te, sVEa, f"p15.rT.{oc - 2}")
                mm = None
                for icx in range(IC):
                    for q2 in range(2):
                        te.matmul(pS[:, s, q2 * 512:(q2 + 1) * 512],
                                  sgnc[:, s][:, icx, :],
                                  xhi[:, icx, q2 * 512:(q2 + 1) * 512],
                                  start=(icx == 0), stop=False)
                    for q2 in range(2):
                        mm = te.matmul(pS[:, s, q2 * 512:(q2 + 1) * 512],
                                       sgnc[:, s][:, icx, :],
                                       xlo[:, icx, q2 * 512:(q2 + 1) * 512],
                                       start=False, stop=(icx == IC - 1))
                inc(mm, sPEb, 1, mark=f"p15.S.{oc}")

        P.run_phase({"sync": p15_sync, "scalar": p15_sc, "vector": p15_ve,
                     "tensor": p15_te})

    mid.close()

    with ExitStack() as ph:
        w_h2 = ph.enter_context(nc.sbuf_tensor("w_h2", [128, 2, T], F32))
        sel = ph.enter_context(nc.sbuf_tensor("sel", [16, 2048], F32))
        wtok = ph.enter_context(nc.sbuf_tensor("wtok", [128, 2, 16, 128], F32))
        vf = ph.enter_context(nc.sbuf_tensor("vf", [128, 16, 128], F32))
        khi = ph.enter_context(nc.sbuf_tensor("khi", [128, 2, T], BF16))
        klo = ph.enter_context(nc.sbuf_tensor("klo", [128, 2, T], BF16))
        qf = ph.enter_context(nc.sbuf_tensor("qf", [128, T], F32))
        qhi = ph.enter_context(nc.sbuf_tensor("qhi", [128, T], BF16))
        qlo = ph.enter_context(nc.sbuf_tensor("qlo", [128, T], BF16))
        vhi = ph.enter_context(nc.sbuf_tensor("vhi", [128, 16, 128], BF16))
        vlo = ph.enter_context(nc.sbuf_tensor("vlo", [128, 16, 128], BF16))
        vphi = ph.enter_context(nc.sbuf_tensor("vphi", [128, 16, 128], BF16))
        vplo = ph.enter_context(nc.sbuf_tensor("vplo", [128, 16, 128], BF16))
        ehi = ph.enter_context(nc.sbuf_tensor("ehi", [128, 16, TH], BF16))
        elo = ph.enter_context(nc.sbuf_tensor("elo", [128, 16, TH], BF16))
        ef32 = ph.enter_context(nc.sbuf_tensor("ef32", [128, 4, TH], F32))
        dump = ph.enter_context(nc.sbuf_tensor("dump", [128, TH], F32))
        ms = ph.enter_context(nc.sbuf_tensor("ms", [128, 2, 16], F32))
        msr = ph.enter_context(nc.sbuf_tensor("msr", [128, 2, 16], F32))
        mst = ph.enter_context(nc.sbuf_tensor("mst", [128, 2, 16], F32))
        rsqh = ph.enter_context(nc.sbuf_tensor("rsqh", [128, 2, 16], F32))
        rsqTs = ph.enter_context(nc.sbuf_tensor("rsqTs", [16, 128], F32))
        zm = ph.enter_context(nc.sbuf_tensor("zm", [128, 16], F32))
        zo = ph.enter_context(nc.sbuf_tensor("zo", [128, 16], F32))
        zinv = ph.enter_context(nc.sbuf_tensor("zinv", [128, 16], F32))
        zTs = ph.enter_context(nc.sbuf_tensor("zTs", [16, 128], F32))
        zbc = ph.enter_context(nc.sbuf_tensor("zbc", [128, TH], F32))
        yt = ph.enter_context(nc.sbuf_tensor("yt", [128, TH], F32))
        yf = ph.enter_context(nc.sbuf_tensor("yf", [128, TH], F32))
        yhib = ph.enter_context(nc.sbuf_tensor("yhib", [128, TH], BF16))
        ylob = ph.enter_context(nc.sbuf_tensor("ylob", [128, TH], BF16))
        msdump = ph.enter_context(nc.sbuf_tensor("msdump", [128, 128], F32))
        patt0 = ph.enter_context(nc.psum_tensor("patt0", [128, TH], F32))
        patt1 = ph.enter_context(nc.psum_tensor("patt1", [128, TH], F32))
        patt2 = ph.enter_context(nc.psum_tensor("patt2", [128, TH], F32))
        patt3 = ph.enter_context(nc.psum_tensor("patt3", [128, TH], F32))
        patt = [patt0, patt1, patt2, patt3]

        def p2_sync(sy):
            def ystores(h):
                w(sy, sVEa, f"p2.yc.{h}")
                inc(sy.dma_start(out=yhsc[h], in_=yhib[:]), sYST, 16)
                inc(sy.dma_start(out=ylsc[h], in_=ylob[:]), sYST, 16,
                    mark=f"p2.yst.{h}")

            for h in range(H):
                s = h % 2
                if h == 0:
                    w(sy, sVEa, "p15.rT.15")
                    inc(sy.dma_start(out=sel[:], in_=seld), sCONST, 16,
                        mark="p2.selld")
                    w(sy, sWGF[0], "p15.wgf.14")
                    w(sy, sWGF[1], "p15.wgf.15")
                    w(sy, sRST[0], "p15.rst.14")
                    w(sy, sRST[1], "p15.rst.15")
                if h >= 2:
                    w(sy, sVEa, f"p2.whfree.{h - 2}")
                w(sy, sWST, f"p1.wst.{h}")
                inc(sy.dma_start(out=w_h2[:, s, :], in_=wsc[h]), sWLD[s], 16,
                    mark=f"p2.wld.{h}")
                if h >= 2:
                    w(sy, sACa, f"p2.ms.{h - 2}.15")
                    w(sy, sVEa, f"p2.vstt.{h - 2}")
                w(sy, sWTT, f"p1.wtt.{h}")
                inc(sy.dma_start(out=wtok[:, s], in_=wscT[h]), sWTK[s], 16,
                    mark=f"p2.wtk.{h}")
                if dbg and h == 0:
                    w(sy, sVEa, f"p2.rms1.{h}")
                    inc(sy.dma_start(out=dbg_ms, in_=ms[:, :]), sDBG, 16)
                    inc(sy.dma_start(out=dbg_msr, in_=msr[:, :]), sDBG, 16)
                    inc(sy.dma_start(out=dbg_mst, in_=mst[:, :]), sDBG, 16,
                        mark="dbg.mst")
                    w(sy, sACa, f"p2.sqrt.{h}")
                    inc(sy.dma_start(out=dbg_sq0, in_=rsqh[:, :]), sDBG, 16,
                        mark="dbg.sq0")
                    w(sy, sVEa, f"p2.rsq.{h}")
                    inc(sy.dma_start(out=dbg_rsq, in_=rsqh[:, :]), sDBG, 16,
                        mark="dbg.rsq")
                    w(sy, sVEa, f"p2.qk.{h}")
                    inc(sy.dma_start(out=dbg_qf, in_=qf[:, :]), sDBG, 16)
                    inc(sy.dma_start(out=dbg_khi, in_=khi[:, :]), sDBG, 16)
                    inc(sy.dma_start(out=dbg_wtok, in_=wtok[:, :, :]), sDBG, 16)
                    w(sy, sVEa, f"p2.v.{h}")
                    inc(sy.dma_start(out=dbg_vf, in_=vf[:, :, :]), sDBG, 16)
                    w(sy, sVEa, f"p2.es.{h}.15")
                    inc(sy.dma_start(out=dbg_ehi, in_=ehi[:, :, :]), sDBG, 16)
                    inc(sy.dma_start(out=dbg_elo, in_=elo[:, :, :]), sDBG, 16)
                    w(sy, sVEb, f"p2.z.{h}")
                    inc(sy.dma_start(out=dbg_z, in_=zm[:, :]), sDBG, 16)
                    inc(sy.dma_start(out=dbg_zi, in_=zinv[:, :]), sDBG, 16)
                    w(sy, sVEa, f"p2.zbce.{h}")
                    inc(sy.dma_start(out=dbg_zbc, in_=zbc[:, :]), sDBG, 16)
                if h >= 1:
                    ystores(h - 1)
            ystores(H - 1)

        def p2_te(te):
            for h in range(H):
                s = h % 2
                w(te, sWLD[s], f"p2.wld.{h}")
                w(te, sVEa, f"p2.rsq.{h}")
                if h >= 1:
                    w(te, sVEa, f"p2.zTe.{h - 1}")
                tt = te.transpose(patt2[0:16, 0:128], rsqh[:, h % 2, :], ident[:])
                inc(tt, sPEa, 1, mark=f"p2.rsqT.{h}")
                if h == 0:
                    w(te, sCONST, "p2.selld")
                if h >= 1:
                    w(te, sVEa, f"p2.zbce.{h - 1}")
                w(te, sVEa, f"p2.rsqTe.{h}")
                for tc in range(16):
                    if tc >= 2:
                        w(te, sVEa, f"p2.pbcf.{h}.{tc - 2}")
                    mm = te.matmul(patt[2 + tc % 2][:, 512:640],
                                   sel[:, tc * 128:(tc + 1) * 128],
                                   rsqTs[:, :], start=True, stop=True)
                    inc(mm, sPEa, 1, mark=f"p2.sel.{h}.{tc}")
                w(te, sVEa, f"p2.qk.{h}")
                if h >= 1:
                    w(te, sVEa, f"p2.yc.{h - 1}")
                for jc in range(16):
                    for ih in range(2):
                        kstep = jc * 2 + ih
                        if kstep >= 4:
                            w(te, sACb, f"p2.exp.{h}.{kstep - 4}")
                        mm = None
                        for q2 in range(2):
                            mm = te.matmul(
                                patt[kstep % 4][:, q2 * 512:(q2 + 1) * 512],
                                khi[:, h % 2, jc * 128:(jc + 1) * 128],
                                qhi[:, ih * TH + q2 * 512: ih * TH + (q2 + 1) * 512],
                                start=True, stop=False)
                        for q2 in range(2):
                            mm = te.matmul(
                                patt[kstep % 4][:, q2 * 512:(q2 + 1) * 512],
                                khi[:, h % 2, jc * 128:(jc + 1) * 128],
                                qlo[:, ih * TH + q2 * 512: ih * TH + (q2 + 1) * 512],
                                start=False, stop=False)
                        for q2 in range(2):
                            mm = te.matmul(
                                patt[kstep % 4][:, q2 * 512:(q2 + 1) * 512],
                                klo[:, h % 2, jc * 128:(jc + 1) * 128],
                                qhi[:, ih * TH + q2 * 512: ih * TH + (q2 + 1) * 512],
                                start=False, stop=(True))
                        inc(mm, sPEb, 1, mark=f"p2.att.{h}.{kstep}")
                w(te, sACb, f"p2.exp.{h}.28")
                mm = None
                for jc in range(16):
                    w(te, sVEa, f"p2.es.{h}.{jc}")
                    for q2 in range(2):
                        te.matmul(patt0[:, q2 * 512:(q2 + 1) * 512],
                                  vhi[:, jc, :],
                                  ehi[:, jc, q2 * 512:(q2 + 1) * 512],
                                  start=(jc == 0), stop=False)
                    for q2 in range(2):
                        te.matmul(patt0[:, q2 * 512:(q2 + 1) * 512],
                                  vhi[:, jc, :],
                                  elo[:, jc, q2 * 512:(q2 + 1) * 512],
                                  start=False, stop=False)
                    for q2 in range(2):
                        mm = te.matmul(patt0[:, q2 * 512:(q2 + 1) * 512],
                                       vlo[:, jc, :],
                                       ehi[:, jc, q2 * 512:(q2 + 1) * 512],
                                       start=False, stop=(jc == 15))
                inc(mm, sPEb, 1, mark=f"p2.y1.{h}")
                w(te, sVEa, f"p2.vp.{h}")
                w(te, sACb, f"p2.exp.{h}.29")
                mm = None
                for jc in range(16):
                    for q2 in range(2):
                        te.matmul(patt1[:, q2 * 512:(q2 + 1) * 512],
                                  vphi[:, jc, :],
                                  ehi[:, jc, q2 * 512:(q2 + 1) * 512],
                                  start=(jc == 0), stop=False)
                    for q2 in range(2):
                        te.matmul(patt1[:, q2 * 512:(q2 + 1) * 512],
                                  vphi[:, jc, :],
                                  elo[:, jc, q2 * 512:(q2 + 1) * 512],
                                  start=False, stop=False)
                    for q2 in range(2):
                        mm = te.matmul(patt1[:, q2 * 512:(q2 + 1) * 512],
                                       vplo[:, jc, :],
                                       ehi[:, jc, q2 * 512:(q2 + 1) * 512],
                                       start=False, stop=(jc == 15))
                inc(mm, sPEb, 1, mark=f"p2.y.{h}")
                if h + 1 < H:
                    for _ in range(120):
                        te.ldweights(ehi[:, 0, 0:128])
                w(te, sVEb, f"p2.z.{h}")
                w(te, sACb, f"p2.exp.{h}.31")
                tt = te.transpose(patt2[0:16, 0:128], zinv[:, :], ident[:])
                inc(tt, sPEa, 1, mark=f"p2.zT.{h}")
                w(te, sVEa, f"p2.zTe.{h}")
                for tc in range(8):
                    if tc >= 2:
                        w(te, sVEa, f"p2.zbcev.{h}.{tc - 2}")
                    mm = te.matmul(patt[2 + tc % 2][:, 640:768],
                                   sel[:, tc * 128:(tc + 1) * 128],
                                   zTs[:, :], start=True, stop=True)
                    inc(mm, sPEa, 1, mark=f"p2.zsel.{h}.{tc}")

        def p2_sc(sc):
            def mschain(h):
                s2 = h % 2
                w(sc, sWTK[s2], f"p2.wtk.{h}")
                for tc in range(16):
                    inc(sc.activation(msdump[:], wtok[:, s2, tc, :], AF.Square,
                                      accum_out=ms[:, s2, tc:tc + 1]),
                        sACa, 1, mark=f"p2.ms.{h}.{tc}")
                w(sc, sVEa, f"p2.rms1.{h}")
                if dbg and h == 0:
                    w(sc, sDBG, "dbg.mst")
                inc(sc.activation(rsqh[:, s2, :], mst[:, s2, :], AF.Sqrt),
                    sACa, 1, mark=f"p2.sqrt.{h}")

            for h in range(H):
                s2 = h % 2
                if h == 0:
                    mschain(0)
                if h >= 1:
                    w(sc, sVEb, f"p2.z.{h - 1}")
                for jc in range(16):
                    for ih in range(2):
                        kstep = jc * 2 + ih
                        w(sc, sPEb, f"p2.att.{h}.{kstep}")
                        aguard(sc)
                        if ih == 0:
                            if jc >= 4:
                                w(sc, sVEa, f"p2.es.{h}.{jc - 4}")
                            i1 = sc.activation(ef32[:, jc % 4], patt[kstep % 4][:, :],
                                               AF.Exp, bias=cshift[:, 0:1],
                                               scale=rsqh[:, s2, jc:jc + 1],
                                               accum_out=zm[:, jc:jc + 1])
                        else:
                            i1 = sc.activation(dump[:], patt[kstep % 4][:, :],
                                               AF.Exp, bias=cshift[:, 0:1],
                                               scale=rsqh[:, s2, jc:jc + 1],
                                               accum_out=zo[:, jc:jc + 1])
                        inc(i1, sACb, 1, mark=f"p2.exp.{h}.{kstep}")
                if h + 1 < H:
                    mschain(h + 1)

        def p2_ve(ve):
            def rmschain(h):
                s2 = h % 2
                w(ve, sACa, f"p2.ms.{h}.15")
                inc(ve.tensor_scalar(msr[:, s2, :], ms[:, s2, :], INV_DH, EPS_RMS,
                                     ALU.mult, ALU.add), sVEa, 1,
                    mark=f"p2.msr.{h}")
                w(ve, sVEa, f"p2.msr.{h}")
                inc(ve.reciprocal(mst[:, s2, :], msr[:, s2, :]), sVEa, 1,
                    mark=f"p2.rms1.{h}")
                w(ve, sACa, f"p2.sqrt.{h}")
                if dbg and h == 0:
                    w(ve, sDBG, "dbg.sq0")
                for it in range(2):
                    inc(ve.tensor_mul(mst[:, s2, :], rsqh[:, s2, :], rsqh[:, s2, :]),
                        sVEa, 1, mark=f"p2.n{it}a.{h}")
                    w(ve, sVEa, f"p2.n{it}a.{h}")
                    inc(ve.tensor_mul(mst[:, s2, :], mst[:, s2, :], msr[:, s2, :]),
                        sVEa, 1, mark=f"p2.n{it}b.{h}")
                    w(ve, sVEa, f"p2.n{it}b.{h}")
                    inc(ve.tensor_scalar(mst[:, s2, :], mst[:, s2, :], -0.5, 1.5,
                                         ALU.mult, ALU.add), sVEa, 1,
                        mark=f"p2.n{it}c.{h}")
                    w(ve, sVEa, f"p2.n{it}c.{h}")
                    inc(ve.tensor_mul(rsqh[:, s2, :], rsqh[:, s2, :], mst[:, s2, :]),
                        sVEa, 1, mark=f"p2.n{it}d.{h}")
                    w(ve, sVEa, f"p2.n{it}d.{h}")
                inc(ve.tensor_copy(mst[:, s2, :], rsqh[:, s2, :]), sVEa, 1,
                    mark=f"p2.rsq.{h}")

            def kcasts(h):
                s2 = h % 2
                w(ve, sWLD[s2], f"p2.wld.{h}")
                ve.tensor_copy(khi[:, s2], w_h2[:, s2, :])
                i1 = ve.tensor_tensor(klo[:, s2], w_h2[:, s2, :], khi[:, s2],
                                      ALU.subtract)
                inc(i1, sVEa, 1, mark=f"p2.kc.{h}")

            for h in range(H):
                s = h % 2
                if h == 0:
                    rmschain(0)
                    kcasts(0)
                w(ve, sPEa, f"p2.rsqT.{h}")
                vguard(ve)
                inc(ve.tensor_copy(rsqTs[:, :], patt2[0:16, 0:128]), sVEa, 1,
                    mark=f"p2.rsqTe.{h}")
                for tc in range(16):
                    w(ve, sPEa, f"p2.sel.{h}.{tc}")
                    vguard(ve)
                    i1 = ve.scalar_tensor_tensor(qf[:, tc * 128:(tc + 1) * 128],
                                                 w_h2[:, s, tc * 128:(tc + 1) * 128],
                                                 gg[:, 0:1],
                                                 patt[2 + tc % 2][:, 512:640],
                                                 ALU.mult, ALU.mult)
                    inc(i1, sVEa, 1, mark=f"p2.pbcf.{h}.{tc}")
                w(ve, sVEa, f"p2.pbcf.{h}.15")
                ve.tensor_copy(qhi[:], qf[:])
                i1 = ve.tensor_tensor(qlo[:], qf[:], qhi[:], ALU.subtract)
                inc(i1, sVEa, 2)
                mark(f"p2.qk.{h}", sVEa)
                mark(f"p2.whfree.{h}", sVEa)
                for tc in range(16):
                    i1 = ve.scalar_tensor_tensor(vf[:, tc, :],
                                                 wtok[:, s, tc, :],
                                                 rsqh[:, s, tc:tc + 1], g3bc[:],
                                                 ALU.mult, ALU.mult)
                inc(i1, sVEa, 1, mark=f"p2.vstt.{h}")
                w(ve, sVEa, f"p2.vstt.{h}")
                ve.tensor_copy(vhi[:], vf[:])
                inc(ve.tensor_tensor(vlo[:], vf[:], vhi[:], ALU.subtract),
                    sVEa, 1, mark=f"p2.v.{h}")
                for jc in range(16):
                    w(ve, sACb, f"p2.exp.{h}.{jc * 2}")
                    ve.tensor_copy(ehi[:, jc, :], ef32[:, jc % 4])
                    i1 = ve.tensor_tensor(elo[:, jc, :], ef32[:, jc % 4],
                                          ehi[:, jc, :], ALU.subtract)
                    inc(i1, sVEa, 1, mark=f"p2.es.{h}.{jc}")
                w(ve, sACb, f"p2.exp.{h}.31")
                inc(ve.tensor_add(zm[:, :], zm[:, :], zo[:, :]), sVEa, 1,
                    mark=f"p2.zsum.{h}")
                w(ve, sVEa, f"p2.zsum.{h}")
                inc(ve.reciprocal(zinv[:, :], zm[:, :]), sVEb, 1,
                    mark=f"p2.z.{h}")
                w(ve, sVEb, f"p2.z.{h}")
                for tc in range(16):
                    ve.tensor_single_scalar(vphi[:, tc, :], vf[:, tc, :],
                                            zinv[:, tc:tc + 1], ALU.mult)
                    ve.scalar_tensor_tensor(vplo[:, tc, :], vf[:, tc, :],
                                            zinv[:, tc:tc + 1], vphi[:, tc, :],
                                            ALU.mult, ALU.subtract)
                i1 = ve.tensor_copy(msdump[:, 0:1], zinv[:, 0:1])
                inc(i1, sVEa, 1, mark=f"p2.vp.{h}")
                if h + 1 < H:
                    rmschain(h + 1)
                    kcasts(h + 1)
                w(ve, sPEa, f"p2.zT.{h}")
                vguard(ve)
                inc(ve.tensor_copy(zTs[:, :], patt2[0:16, 0:128]), sVEa, 1,
                    mark=f"p2.zTe.{h}")
                for tc in range(8):
                    w(ve, sPEa, f"p2.zsel.{h}.{tc}")
                    vguard(ve)
                    i1 = ve.tensor_copy(zbc[:, tc * 128:(tc + 1) * 128],
                                        patt[2 + tc % 2][:, 640:768])
                    inc(i1, sVEa, 1, mark=f"p2.zbcev.{h}.{tc}")
                mark(f"p2.zbce.{h}", sVEa)
                w(ve, sPEb, f"p2.y.{h}")
                if h >= 1:
                    w(ve, sYST, f"p2.yst.{h - 1}")
                vguard(ve)
                ve.tensor_tensor(yt[:], patt0[:, :], zbc[:], ALU.mult)
                inc(ve.tensor_add(yf[:], yt[:], patt1[:, :]), sVEa, 1,
                    mark=f"p2.yf.{h}")
                w(ve, sVEa, f"p2.yf.{h}")
                ve.tensor_copy(yhib[:], yf[:])
                i1 = ve.tensor_tensor(ylob[:], yf[:], yhib[:], ALU.subtract)
                inc(i1, sVEa, 1, mark=f"p2.yc.{h}")

        P.run_phase({"sync": p2_sync, "tensor": p2_te, "scalar": p2_sc,
                     "vector": p2_ve})

    with ExitStack() as ph:
        ybh = ph.enter_context(nc.sbuf_tensor("ybh", [128, H, TH], BF16))
        ybl = ph.enter_context(nc.sbuf_tensor("ybl", [128, H, TH], BF16))
        wf3 = ph.enter_context(nc.sbuf_tensor("wf3", [128, 2, TH], F32))
        whih = ph.enter_context(nc.sbuf_tensor("whih", [128, 16, TH], BF16))
        wloh = ph.enter_context(nc.sbuf_tensor("wloh", [128, 16, TH], BF16))
        rfb = ph.enter_context(nc.sbuf_tensor("rfb", [128, 2, TH], F32))
        resb = ph.enter_context(nc.sbuf_tensor("resb", [128, 2, TH], F32))
        xhb = ph.enter_context(nc.sbuf_tensor("xhb", [128, 2, TH], F32))
        outb = ph.enter_context(nc.sbuf_tensor("outb", [128, 2, TH], F32))
        attcp = ph.enter_context(nc.sbuf_tensor("attcp", [128, TH], F32)) if dbg else None
        pout0 = ph.enter_context(nc.psum_tensor("pout0", [128, TH], F32))
        pout1 = ph.enter_context(nc.psum_tensor("pout1", [128, TH], F32))
        ptr3A = ph.enter_context(nc.psum_tensor("ptr3A", [128, 512], F32))
        ptr3B = ph.enter_context(nc.psum_tensor("ptr3B", [128, 512], F32))
        pout = [pout0, pout1]
        ptr3 = [ptr3A, ptr3B]

        def p3_sync(sy):
            w(sy, sPEb, f"p2.y.{H - 1}")
            w(sy, sWLD[0], "p2.wld.14")
            w(sy, sWLD[1], "p2.wld.15")
            for h in range(H):
                w(sy, sYST, f"p2.yst.{h}")
                inc(sy.dma_start(out=ybh[:, h, :], in_=yhsc[h]), sP3Y, 16)
                inc(sy.dma_start(out=ybl[:, h, :], in_=ylsc[h]), sP3Y, 16,
                    mark=f"p3.yld.{h}")
            for ch in range(2):
                for fc in range(16):
                    s = fc % 2
                    if fc >= 2 or ch > 0:
                        prev = (f"p3.wlo.{ch}.{fc - 2}" if fc >= 2 else
                                f"p3.wlo.{ch - 1}.{14 + fc}")
                        w(sy, sVEa, prev)
                    inc(sy.dma_start(out=wf3[:, s],
                                     in_=Wnv[:, fc, ch * TH:(ch + 1) * TH]),
                        sWF[s], 16, mark=f"p3.wf.{ch}.{fc}")
                for tc in range(8):
                    s = tc % 2
                    if tc >= 2 or ch > 0:
                        prev = (f"p3.oe.{ch}.{tc - 2}" if tc >= 2 else
                                f"p3.oe.{ch - 1}.{6 + tc}")
                        w(sy, sVEa, prev)
                    for q8 in range(8):
                        inc(sy.dma_start(
                            out=rfb[:, s][:, q8 * 128:(q8 + 1) * 128],
                            in_=resFv[:, ch * 8 + q8, tc * 128:(tc + 1) * 128]),
                            sRX[s], 16)
                    inc(sy.dma_start(out=xhb[:, s],
                                     in_=xhv[:, tc, ch * TH:(ch + 1) * TH]),
                        sRX[s], 16, mark=f"p3.rx.{ch}.{tc}")
                    w(sy, sVEa, f"p3.oe.{ch}.{tc}")
                    if dbg and ch == 1 and tc == 0:
                        inc(sy.dma_start(out=dbg_att, in_=attcp[:]), sDBG, 16)
                        inc(sy.dma_start(out=dbg_resb, in_=resb[:, 0]), sDBG, 16)
                        inc(sy.dma_start(out=dbg_xhb2, in_=xhb[:, 0]), sDBG, 16,
                            mark="dbg.p3")
                    inc(sy.dma_start(out=outv[:, tc, ch * TH:(ch + 1) * TH],
                                     in_=outb[:, s]), sOST[s], 16,
                        mark=f"p3.ost.{ch}.{tc}")
            w(sy, sOST[0], "p3.ost.1.6")
            w(sy, sOST[1], "p3.ost.1.7")

        def p3_gp(g):
            for ch in range(2):
                if ch == 0:
                    w(g, sPEb, f"p2.y.{H - 1}")
                    w(g, sWLD[0], "p2.wld.14")
                    w(g, sWLD[1], "p2.wld.15")
                    w(g, sWTH[0], "p1.wthi.14")
                    w(g, sWTH[1], "p1.wthi.15")
                if ch > 0:
                    w(g, sPEb, f"p3.mm.{ch - 1}.7")
                for fc in range(16):
                    inc(g.dma_start(out=whih[:, fc, :],
                                    in_=Wnv[:, fc, ch * TH:(ch + 1) * TH]),
                        sWHIF[fc], 16, mark=f"p3.whi.{ch}.{fc}")

        def p3_te(te):
            for ch in range(2):
                for tc in range(8):
                    sp = tc % 2
                    if tc >= 2 or ch > 0:
                        prev = (f"p3.oe.{ch}.{tc - 2}" if tc >= 2 else
                                f"p3.oe.{ch - 1}.{6 + tc}")
                        w(te, sVEa, prev)
                    for h in range(H):
                        w(te, sVEa, f"p3.wlo.{ch}.{h}")
                        w(te, sP3Y, f"p3.yld.{h}")
                        for q2 in range(2):
                            te.matmul(pout[sp][:, q2 * 512:(q2 + 1) * 512],
                                      ybh[:, h, tc * 128:(tc + 1) * 128],
                                      whih[:, h, q2 * 512:(q2 + 1) * 512],
                                      start=(h == 0), stop=False)
                        for q2 in range(2):
                            te.matmul(pout[sp][:, q2 * 512:(q2 + 1) * 512],
                                      ybh[:, h, tc * 128:(tc + 1) * 128],
                                      wloh[:, h, q2 * 512:(q2 + 1) * 512],
                                      start=False, stop=False)
                        for q2 in range(2):
                            mm = te.matmul(pout[sp][:, q2 * 512:(q2 + 1) * 512],
                                           ybl[:, h, tc * 128:(tc + 1) * 128],
                                           whih[:, h, q2 * 512:(q2 + 1) * 512],
                                           start=False, stop=(h == H - 1))
                    inc(mm, sPEb, 1, mark=f"p3.mm.{ch}.{tc}")
                    w(te, sRX[tc % 2], f"p3.rx.{ch}.{tc}")
                    for q8 in range(8):
                        if q8 >= 2:
                            w(te, sVEa, f"p3.rtev.{ch}.{tc}.{q8 - 2}")
                        elif tc >= 1 or ch > 0:
                            prev = (f"p3.rtev.{ch}.{tc - 1}.{6 + q8}" if tc >= 1
                                    else f"p3.rtev.{ch - 1}.{7}.{6 + q8}")
                            w(te, sVEa, prev)
                        tt = te.transpose(ptr3[q8 % 2][:, 0:128],
                                          rfb[:, tc % 2][:, q8 * 128:(q8 + 1) * 128],
                                          ident[:])
                        inc(tt, sPEa, 1, mark=f"p3.rt.{ch}.{tc}.{q8}")

        def p3_ve(ve):
            for ch in range(2):
                if ch > 0:
                    w(ve, sPEb, f"p3.mm.{ch - 1}.7")
                for fc in range(16):
                    s = fc % 2
                    w(ve, sWF[s], f"p3.wf.{ch}.{fc}")
                    w(ve, sWHIF[fc], f"p3.whi.{ch}.{fc}")
                    i1 = ve.tensor_tensor(wloh[:, fc, :], wf3[:, s],
                                          whih[:, fc, :], ALU.subtract)
                    inc(i1, sVEa, 1, mark=f"p3.wlo.{ch}.{fc}")
                for tc in range(8):
                    sp = tc % 2
                    if dbg and ch == 1 and tc == 2:
                        w(ve, sDBG, "dbg.p3")
                    for q8 in range(8):
                        w(ve, sPEa, f"p3.rt.{ch}.{tc}.{q8}")
                        vguard(ve)
                        i1 = ve.tensor_copy(resb[:, sp][:, q8 * 128:(q8 + 1) * 128],
                                            ptr3[q8 % 2][:, 0:128])
                        inc(i1, sVEa, 1, mark=f"p3.rtev.{ch}.{tc}.{q8}")
                    w(ve, sPEb, f"p3.mm.{ch}.{tc}")
                    if tc >= 2 or ch > 0:
                        prev = (f"p3.ost.{ch}.{tc - 2}" if tc >= 2 else
                                f"p3.ost.{ch - 1}.{6 + tc}")
                        w(ve, sOST[tc % 2], prev)
                    w(ve, sRX[tc % 2], f"p3.rx.{ch}.{tc}")
                    vguard(ve)
                    if dbg and ch == 1 and tc == 0:
                        inc(ve.tensor_copy(attcp[:], pout[sp][:, :]), sVEa, 1,
                            mark="dbg.attcp")
                    ve.tensor_tensor(outb[:, sp], pout[sp][:, :], resb[:, sp],
                                     ALU.add)
                    i1 = ve.tensor_add(outb[:, sp], outb[:, sp], xhb[:, sp])
                    inc(i1, sVEa, 1, mark=f"p3.oe.{ch}.{tc}")

        P.run_phase({"sync": p3_sync, "gpsimd": p3_gp, "tensor": p3_te,
                     "vector": p3_ve})

    top.close()
    return nc


_NC_CACHE = {}


def kernel(L0_attn_x, L0_attn_W, L0_attn_g1, L0_attn_g2, L0_attn_g3,
           L0_ffn_weight, L0_ffn_bias):
    x = np.ascontiguousarray(np.asarray(L0_attn_x, dtype=np.float32))
    W = np.ascontiguousarray(np.asarray(L0_attn_W, dtype=np.float32))
    g1 = np.asarray(L0_attn_g1, dtype=np.float32)
    g2 = np.asarray(L0_attn_g2, dtype=np.float32)
    g3 = np.asarray(L0_attn_g3, dtype=np.float32)
    wgt = np.ascontiguousarray(np.asarray(L0_ffn_weight, dtype=np.float32))
    bias = np.asarray(L0_ffn_bias, dtype=np.float32)

    B = x.shape[0]
    WTc = np.ascontiguousarray(W.T)
    wgtTc = np.ascontiguousarray(wgt.T)
    biastc = np.ascontiguousarray(bias.reshape(16, 128).T)
    g1c = np.ascontiguousarray(g1.reshape(128, 1))
    g2c = np.ascontiguousarray(g2.reshape(128, 1))
    g3bcc = np.ascontiguousarray(np.broadcast_to(g3.reshape(1, 128), (128, 128)))
    identc = np.eye(128, dtype=np.float32)
    selc = np.zeros((16, 2048), dtype=np.float32)
    for k in range(16):
        selc[k, k * 128:(k + 1) * 128] = 1.0
    aonesc = np.ones((128, 128), dtype=np.float32)
    negcbfc = np.full((128, 1), -1.0, dtype=ml_dtypes.bfloat16)
    onescc = np.ones((128, 1), dtype=np.float32)
    cshiftc = np.full((128, 1), C_SHIFT, dtype=np.float32)

    in_maps = []
    for c in range(8):
        b = c // 2
        half = c % 2
        mine = slice(half * TH, (half + 1) * TH)
        other = slice((1 - half) * TH, (2 - half) * TH)
        xp = np.concatenate([x[b][mine], x[b][other]], axis=0)
        in_maps.append({
            "xT": np.ascontiguousarray(xp.T),
            "xhd": np.ascontiguousarray(x[b][mine]),
            "WTm": WTc, "Wn": W, "wgtT": wgtTc, "biastd": biastc,
            "g1d": g1c, "g2d": g2c, "g3bcd": g3bcc, "identd": identc,
            "seld": selc, "aonesd": aonesc, "negcbfd": negcbfc,
            "onescd": onescc, "cshiftd": cshiftc,
        })

    if "nc" not in _NC_CACHE:
        _NC_CACHE["nc"] = build_nc()
    nc = _NC_CACHE["nc"]
    r = run_bass_kernel_spmd(nc, in_maps, core_ids=list(range(8)))
    outp = np.empty((B, T, D), dtype=np.float32)
    for c in range(8):
        b = c // 2
        half = c % 2
        outp[b, half * TH:(half + 1) * TH, :] = r.results[c]["out"]
    return outp


# version 24 (final)
# speedup vs baseline: 1.0279x; 1.0279x over previous
import math
import numpy as np
import ml_dtypes
import concourse.bass as bass
import concourse.mybir as mybir
from concourse.bass_utils import run_bass_kernel_spmd
from contextlib import ExitStack

F32 = mybir.dt.float32
BF16 = mybir.dt.bfloat16
AF = mybir.ActivationFunctionType
ALU = mybir.AluOpType

T = 2048
D = 2048
H = 16
TH = 1024
IC = 16
EPS_LN = 1e-5
EPS_RMS = float(np.float32(2.0 ** -23))
INV_D = 1.0 / 2048.0
INV_DH = 1.0 / 128.0
GG_SCALE = 1.0 / math.sqrt(128.0)
C_SHIFT = -12.0
ENGS = ("sync", "gpsimd", "tensor", "vector", "scalar")


class SemT:
    def __init__(self, handle):
        self.h = handle
        self.c = 0


class _DryInst:
    def then_inc(self, sem, n=1):
        return self


_DRY = _DryInst()


class DryEng:
    def __getattr__(self, name):
        def _f(*a, **k):
            return _DRY
        return _f


class _RealInst:
    __slots__ = ("i",)

    def __init__(self, inst):
        self.i = inst

    def then_inc(self, sem, n=1):
        self.i.then_inc(sem.h, n)
        return self


class RealEng:
    def __init__(self, eng):
        object.__setattr__(self, "_e", eng)

    def __getattr__(self, name):
        meth = getattr(self._e, name)

        def _f(*a, **k):
            return _RealInst(meth(*a, **k))
        return _f


class Prog:
    def __init__(self, nc):
        self.nc = nc
        self.dry = True
        self.ev = {}

    def mark(self, name, sem):
        if self.dry:
            assert name not in self.ev, name
            self.ev[name] = sem.c

    def inc(self, inst, sem, n=1, mark=None):
        if self.dry:
            sem.c += n
            if mark is not None:
                self.mark(mark, sem)
        else:
            inst.then_inc(sem, n)

    def w(self, eng, sem, name):
        if not self.dry:
            eng._e.wait_ge(sem.h, self.ev[name])

    def wv(self, eng, sem, value):
        if not self.dry:
            eng._e.wait_ge(sem.h, value)

    def run_phase(self, bodies):
        self.dry = True
        for k in ENGS:
            if k in bodies:
                bodies[k](DryEng())
        self.dry = False
        with self.nc.Block() as blk:
            for k in ENGS:
                if k in bodies:
                    fn = bodies[k]
                    getattr(blk, k)(lambda e, fn=fn: fn(RealEng(e)))


def build_nc(dbg=False):
    _ING = "ExternalOutput" if dbg else "Internal"
    nc = bass.Bass("TRN2", target_bir_lowering=False, debug=False,
                   detect_race_conditions=False)

    xT = nc.dram_tensor("xT", [D, T], F32, kind="ExternalInput").ap()
    xhd = nc.dram_tensor("xhd", [TH, D], F32, kind="ExternalInput").ap()
    WT = nc.dram_tensor("WTm", [D, D], F32, kind="ExternalInput").ap()
    Wn = nc.dram_tensor("Wn", [D, D], F32, kind="ExternalInput").ap()
    wgtT = nc.dram_tensor("wgtT", [D, D], F32, kind="ExternalInput").ap()
    biastd = nc.dram_tensor("biastd", [128, 16], F32, kind="ExternalInput").ap()
    g1d = nc.dram_tensor("g1d", [128, 1], F32, kind="ExternalInput").ap()
    g2d = nc.dram_tensor("g2d", [128, 1], F32, kind="ExternalInput").ap()
    g3bcd = nc.dram_tensor("g3bcd", [128, 128], F32, kind="ExternalInput").ap()
    identd = nc.dram_tensor("identd", [128, 128], F32, kind="ExternalInput").ap()
    seld = nc.dram_tensor("seld", [16, 2048], F32, kind="ExternalInput").ap()
    aonesd = nc.dram_tensor("aonesd", [128, 128], F32, kind="ExternalInput").ap()
    negcbfd = nc.dram_tensor("negcbfd", [128, 1], BF16, kind="ExternalInput").ap()
    onescd = nc.dram_tensor("onescd", [128, 1], F32, kind="ExternalInput").ap()
    cshiftd = nc.dram_tensor("cshiftd", [128, 1], F32, kind="ExternalInput").ap()

    out = nc.dram_tensor("out", [TH, D], F32, kind="ExternalOutput").ap()

    wsc = nc.dram_tensor("wsc", [H, 128, T], F32, kind=_ING).ap()
    wscT = nc.dram_tensor("wscT", [H, 128, T], F32, kind=_ING).ap()
    yhsc = nc.dram_tensor("yhsc", [H, 128, TH], BF16, kind=_ING).ap()
    ylsc = nc.dram_tensor("ylsc", [H, 128, TH], BF16, kind=_ING).ap()
    resF = nc.dram_tensor("resF", [D, TH], F32, kind=_ING).ap()
    if dbg:
        dbg_rsq = nc.dram_tensor("dbg_rsq", [128, 16], F32, kind="ExternalOutput").ap()
        dbg_ms = nc.dram_tensor("dbg_ms", [128, 16], F32, kind="ExternalOutput").ap()
        dbg_rbc = nc.dram_tensor("dbg_rbc", [128, T], F32, kind="ExternalOutput").ap()
        dbg_mrbc = nc.dram_tensor("dbg_mrbc", [128, T], F32, kind="ExternalOutput").ap()
        dbg_xlo = nc.dram_tensor("dbg_xlo", [128, IC, T], BF16, kind="ExternalOutput").ap()
        dbg_pw = nc.dram_tensor("dbg_pw", [128, T], F32, kind="ExternalOutput").ap()
        dbg_csn = nc.dram_tensor("dbg_csn", [128, 16], F32, kind="ExternalOutput").ap()
        dbg_msr = nc.dram_tensor("dbg_msr", [128, 16], F32, kind="ExternalOutput").ap()
        dbg_mst = nc.dram_tensor("dbg_mst", [128, 16], F32, kind="ExternalOutput").ap()
        dbg_sq0 = nc.dram_tensor("dbg_sq0", [128, 16], F32, kind="ExternalOutput").ap()
        dbg_z = nc.dram_tensor("dbg_z", [128, 16], F32, kind="ExternalOutput").ap()
        dbg_zi = nc.dram_tensor("dbg_zi", [128, 16], F32, kind="ExternalOutput").ap()
        dbg_ehi = nc.dram_tensor("dbg_ehi", [128, 16, TH], BF16, kind="ExternalOutput").ap()
        dbg_elo = nc.dram_tensor("dbg_elo", [128, 16, TH], BF16, kind="ExternalOutput").ap()
        dbg_qf = nc.dram_tensor("dbg_qf", [128, T], F32, kind="ExternalOutput").ap()
        dbg_khi = nc.dram_tensor("dbg_khi", [128, T], BF16, kind="ExternalOutput").ap()
        dbg_wtok = nc.dram_tensor("dbg_wtok", [128, 16, 128], F32, kind="ExternalOutput").ap()
        dbg_vf = nc.dram_tensor("dbg_vf", [128, 16, 128], F32, kind="ExternalOutput").ap()
        dbg_zbc = nc.dram_tensor("dbg_zbc", [128, TH], F32, kind="ExternalOutput").ap()
        dbg_att = nc.dram_tensor("dbg_att", [128, TH], F32, kind="ExternalOutput").ap()
        dbg_resb = nc.dram_tensor("dbg_resb", [128, TH], F32, kind="ExternalOutput").ap()
        dbg_xhb2 = nc.dram_tensor("dbg_xhb2", [128, TH], F32, kind="ExternalOutput").ap()

    xTv = xT.rearrange("(ic p) t -> p ic t", p=128)
    WTv = WT.rearrange("(ic p) o -> p ic o", p=128)
    wgtTv = wgtT.rearrange("(ic p) o -> p ic o", p=128)
    Wnv = Wn.rearrange("(fc p) c -> p fc c", p=128)
    xhv = xhd.rearrange("(tc p) c -> p tc c", p=128)
    resFv = resF.rearrange("(oc p) t -> p oc t", p=128)
    outv = out.rearrange("(tc p) c -> p tc c", p=128)

    top = ExitStack()
    _sems = ("PEa", "PEb", "VEa", "VEb", "ACa", "ACb",
             "CONST", "XF0", "XF1", "XH0", "XH1", "XH2", "XH3",
             "WTF0", "WTF1", "WTH0", "WTH1", "WST",
             "WGF0", "WGF1", "RST0", "RST1",
             "WLD0", "WLD1", "YST",
             "P3Y", "WF0", "WF1", "RX0", "RX1", "OST0", "OST1", "DBG", "WTT", "WTK0", "WTK1",
             *[f"WHI{i}" for i in range(16)])
    S = {n: SemT(top.enter_context(nc.semaphore("s" + n))) for n in _sems}
    sPEa, sPEb, sVEa, sVEb, sACa, sACb = (S[n] for n in
                                          ("PEa", "PEb", "VEa", "VEb", "ACa", "ACb"))
    sCONST = S["CONST"]
    sXF = [S["XF0"], S["XF1"]]
    sXH = [S["XH0"], S["XH1"], S["XH2"], S["XH3"]]
    sWTF = [S["WTF0"], S["WTF1"]]
    sWTH = [S["WTH0"], S["WTH1"]]
    sWST = S["WST"]
    sWGF = [S["WGF0"], S["WGF1"]]
    sRST = [S["RST0"], S["RST1"]]
    sWLD = [S["WLD0"], S["WLD1"]]
    sYST = S["YST"]
    sP3Y = S["P3Y"]
    sWF = [S["WF0"], S["WF1"]]
    sWHIF = [S[f"WHI{i}"] for i in range(16)]
    sRX = [S["RX0"], S["RX1"]]
    sOST = [S["OST0"], S["OST1"]]
    sDBG = S["DBG"]
    sWTT = S["WTT"]
    sWTK = [S["WTK0"], S["WTK1"]]

    ident = top.enter_context(nc.sbuf_tensor("ident", [128, 128], F32))
    aones = top.enter_context(nc.sbuf_tensor("aones", [128, 128], F32))
    negcbf = top.enter_context(nc.sbuf_tensor("negcbf", [128, 1], BF16))
    onesc = top.enter_context(nc.sbuf_tensor("onesc", [128, 1], F32))
    cshift = top.enter_context(nc.sbuf_tensor("cshift", [128, 1], F32))
    g3bc = top.enter_context(nc.sbuf_tensor("g3bc", [128, 128], F32))
    gg = top.enter_context(nc.sbuf_tensor("gg", [128, 1], F32))
    g1s = top.enter_context(nc.sbuf_tensor("g1s", [128, 1], F32))
    g2s = top.enter_context(nc.sbuf_tensor("g2s", [128, 1], F32))
    biast = top.enter_context(nc.sbuf_tensor("biast", [128, 16], F32))
    vdum = top.enter_context(nc.sbuf_tensor("vdum", [128, 1], F32))
    adum = top.enter_context(nc.sbuf_tensor("adum", [128, 1], F32))

    P = Prog(nc)
    mark, inc, w = P.mark, P.inc, P.w

    def vguard(ve):
        ve.tensor_copy(vdum[:], vdum[:])

    def aguard(sc):
        sc.activation(adum[:], cshift[:], AF.Copy)

    mid = ExitStack()
    xhi = mid.enter_context(nc.sbuf_tensor("xhi", [128, IC, T], BF16))
    xlo = mid.enter_context(nc.sbuf_tensor("xlo", [128, IC, T], BF16))
    rbc = mid.enter_context(nc.sbuf_tensor("rbc", [128, T], F32))
    mrbc = mid.enter_context(nc.sbuf_tensor("mrbc", [128, T], F32))

    with ExitStack() as ph:
        xf = ph.enter_context(nc.sbuf_tensor("xf", [128, 2, 4, 512], F32))
        sq = ph.enter_context(nc.sbuf_tensor("sq", [128, 2, 512], F32))
        mubc = ph.enter_context(nc.sbuf_tensor("mubc", [128, T], F32))
        vtmp = ph.enter_context(nc.sbuf_tensor("vtmp", [128, T], F32))
        vare = ph.enter_context(nc.sbuf_tensor("vare", [128, T], F32))
        pmu = ph.enter_context(nc.psum_tensor("pmu", [128, 512], F32))
        psq = ph.enter_context(nc.psum_tensor("psq", [128, 512], F32))

        def p0_sync(sy):
            for dst, src in [(ident, identd), (aones, aonesd),
                             (negcbf, negcbfd), (onesc, onescd),
                             (cshift, cshiftd), (g3bc, g3bcd), (g1s, g1d),
                             (g2s, g2d), (biast, biastd)]:
                inc(sy.dma_start(out=dst[:], in_=src), sCONST, 16)
            mark("p0.consts", sCONST)
            for tg in range(4):
                for icg in range(4):
                    kk = tg * 4 + icg
                    if kk >= 2:
                        w(sy, sVEa, f"p0.xf_free.{kk - 2}")
                        w(sy, sPEa, f"p0.sqmm.{4 * (kk - 2) + 3}")
                        w(sy, sACa, f"p0.sq.{4 * (kk - 2) + 3}")
                    inc(sy.dma_start(out=xf[:, kk % 2],
                                     in_=xTv[:, icg * 4:(icg + 1) * 4,
                                             tg * 512:(tg + 1) * 512]),
                        sXF[kk % 2], 16, mark=f"p0.xf.{kk}")
            mark("p0.xff0", sXF[0])
            mark("p0.xff1", sXF[1])

        def p0_gp(g):
            for tg in range(4):
                inc(g.dma_start(out=xhi[:, :, tg * 512:(tg + 1) * 512],
                                in_=xTv[:, :, tg * 512:(tg + 1) * 512]),
                    sXH[tg], 16, mark=f"p0.xhi.{tg}")

        def p0_sc(sc):
            w(sc, sCONST, "p0.consts")
            k = 0
            for tg in range(4):
                for icg in range(4):
                    kk = tg * 4 + icg
                    w(sc, sXF[kk % 2], f"p0.xf.{kk}")
                    for icx in range(4):
                        if k >= 2:
                            w(sc, sPEa, f"p0.sqmm.{k - 2}")
                        inc(sc.activation(sq[:, k % 2], xf[:, kk % 2, icx, :],
                                          AF.Square),
                            sACa, 1, mark=f"p0.sq.{k}")
                        k += 1
            w(sc, sVEa, "p0.vare")
            sc.activation(vtmp[:], vare[:], AF.Ln)
            inc(sc.activation(rbc[:], vtmp[:], AF.Exp, scale=-0.5), sACa, 1,
                mark="p0.sqrt")

        def p0_te(te):
            k = 0
            for tg in range(4):
                if tg > 0:
                    w(te, sVEa, f"p0.statev.{tg - 1}")
                for icg in range(4):
                    kk = tg * 4 + icg
                    w(te, sXF[kk % 2], f"p0.xf.{kk}")
                    for icx in range(4):
                        te.matmul(pmu[:, :], aones[:], xf[:, kk % 2, icx, :],
                                  start=(icg == 0 and icx == 0),
                                  stop=(icg == 3 and icx == 3))
                    for icx in range(4):
                        w(te, sACa, f"p0.sq.{k}")
                        mm = te.matmul(psq[:, :], aones[:], sq[:, k % 2],
                                       start=(icg == 0 and icx == 0),
                                       stop=(icg == 3 and icx == 3))
                        inc(mm, sPEa, 1, mark=f"p0.sqmm.{k}")
                        k += 1
                mark(f"p0.stats.{tg}", sPEa)

        def p0_ve(ve):
            ve.memset(vdum[:], 0.0)
            w(ve, sCONST, "p0.consts")
            inc(ve.tensor_mul(gg[:], g1s[:], g2s[:]), sVEa, 1, mark="p0.gg1")
            w(ve, sVEa, "p0.gg1")
            ve.tensor_scalar_mul(gg[:], gg[:], GG_SCALE)
            for tg in range(4):
                w(ve, sXH[tg], f"p0.xhi.{tg}")
                for icg in range(4):
                    kk = tg * 4 + icg
                    w(ve, sXF[kk % 2], f"p0.xf.{kk}")
                    i1 = ve.tensor_tensor(
                        xlo[:, icg * 4:(icg + 1) * 4, tg * 512:(tg + 1) * 512],
                        xf[:, kk % 2],
                        xhi[:, icg * 4:(icg + 1) * 4, tg * 512:(tg + 1) * 512],
                        ALU.subtract)
                    inc(i1, sVEa, 1, mark=f"p0.xf_free.{kk}")
                w(ve, sPEa, f"p0.stats.{tg}")
                vguard(ve)
                ve.tensor_scalar_mul(mubc[:, tg * 512:(tg + 1) * 512],
                                     pmu[:, :], INV_D)
                i2 = ve.tensor_scalar_mul(vare[:, tg * 512:(tg + 1) * 512],
                                          psq[:, :], INV_D)
                inc(i2, sVEa, 1, mark=f"p0.statev.{tg}")
            ve.tensor_mul(vtmp[:], mubc[:], mubc[:])
            ve.tensor_sub(vare[:], vare[:], vtmp[:])
            inc(ve.tensor_scalar_add(vare[:], vare[:], EPS_LN), sVEa, 1,
                mark="p0.vare")
            w(ve, sACa, "p0.sqrt")
            for it in range(1):
                ve.tensor_mul(vtmp[:], rbc[:], rbc[:])
                inc(ve.tensor_mul(vtmp[:], vtmp[:], vare[:]), sVEa, 1,
                    mark=f"p0.nt{it}")
                w(ve, sVEa, f"p0.nt{it}")
                inc(ve.tensor_scalar(vtmp[:], vtmp[:], -0.5, 1.5, ALU.mult,
                                     ALU.add), sVEa, 1, mark=f"p0.nt{it}b")
                w(ve, sVEa, f"p0.nt{it}b")
                ve.tensor_mul(rbc[:], rbc[:], vtmp[:])
            inc(ve.tensor_mul(mrbc[:], mubc[:], rbc[:]), sVEb, 1,
                mark="p0.done")

        P.run_phase({"sync": p0_sync, "gpsimd": p0_gp, "scalar": p0_sc,
                     "tensor": p0_te, "vector": p0_ve})

    with ExitStack() as ph:
        wtf = ph.enter_context(nc.sbuf_tensor("wtf", [128, 2, IC, 128], F32))
        wthi = ph.enter_context(nc.sbuf_tensor("wthi", [128, 2, IC, 128], BF16))
        wtlo = ph.enter_context(nc.sbuf_tensor("wtlo", [128, 2, IC, 128], BF16))
        w_h = ph.enter_context(nc.sbuf_tensor("w_h", [128, 2, T], F32))
        wts = ph.enter_context(nc.sbuf_tensor("wts", [128, 16, 128], F32))
        csn = ph.enter_context(nc.sbuf_tensor("csn", [128, 1], F32))
        pw = ph.enter_context(nc.psum_tensor("pw", [128, T], F32))
        ptw0 = ph.enter_context(nc.psum_tensor("ptw0", [128, 512], F32))
        ptw1 = ph.enter_context(nc.psum_tensor("ptw1", [128, 512], F32))
        ptw2 = ph.enter_context(nc.psum_tensor("ptw2", [128, 512], F32))
        pcs = ph.enter_context(nc.psum_tensor("pcs", [128, 512], F32))

        def p1_sync(sy):
            def wstore(h):
                w(sy, sVEa, f"p1.wh.{h}")
                inc(sy.dma_start(out=wsc[h], in_=w_h[:, h % 2, :]), sWST, 16,
                    mark=f"p1.wst.{h}")

            def wtstore(h):
                w(sy, sVEa, f"p1.wtse.{h}")
                inc(sy.dma_start(out=wscT[h], in_=wts[:, :, :]), sWTT, 16,
                    mark=f"p1.wtt.{h}")

            for h in range(H):
                s = h % 2
                if h == 0:
                    w(sy, sVEb, "p0.done")
                    w(sy, sXF[0], "p0.xff0")
                    w(sy, sXF[1], "p0.xff1")
                    if dbg:
                        inc(sy.dma_start(out=dbg_rbc, in_=rbc[:]), sDBG, 16)
                        inc(sy.dma_start(out=dbg_mrbc, in_=mrbc[:]), sDBG, 16)
                        inc(sy.dma_start(out=dbg_xlo, in_=xlo[:, :, :]), sDBG, 16)
                if h >= 2:
                    w(sy, sPEb, f"p1.proj.{h - 2}")
                inc(sy.dma_start(out=wtf[:, s], in_=WTv[:, :, h * 128:(h + 1) * 128]),
                    sWTF[s], 16, mark=f"p1.wtf.{h}")
                if h >= 1:
                    wstore(h - 1)
                if h >= 2:
                    wtstore(h - 2)
            wstore(H - 1)
            wtstore(H - 2)
            wtstore(H - 1)

        def p1_gp(g):
            for h in range(H):
                s = h % 2
                if h == 0:
                    w(g, sVEb, "p0.done")
                    w(g, sXF[0], "p0.xff0")
                    w(g, sXF[1], "p0.xff1")
                if h >= 2:
                    w(g, sPEb, f"p1.proj.{h - 2}")
                inc(g.dma_start(out=wthi[:, s], in_=WTv[:, :, h * 128:(h + 1) * 128]),
                    sWTH[s], 16, mark=f"p1.wthi.{h}")

        def p1_te(te):
            def trb(te, h):
                w(te, sVEa, f"p1.wh.{h}")
                ptw = [ptw0, ptw1, ptw2]
                for tc in range(16):
                    if tc >= 3:
                        w(te, sVEa, f"p1.wtev.{h}.{tc - 3}")
                    elif h >= 1:
                        w(te, sVEa, f"p1.wtev.{h - 1}.{[15, 13, 14][tc]}")
                    tt = te.transpose(ptw[tc % 3][:, 0:128],
                                      w_h[:, h % 2, tc * 128:(tc + 1) * 128],
                                      ident[:])
                    inc(tt, sPEa, 1, mark=f"p1.tr.{h}.{tc}")

            for h in range(H):
                s = h % 2
                w(te, sVEa, f"p1.wtlo.{h}")
                if h == 0:
                    w(te, sVEb, "p0.done")
                if h >= 1:
                    w(te, sVEa, f"p1.ep1.{h - 1}")
                mm = None
                for icx in range(IC):
                    for tg in range(4):
                        te.matmul(pw[:, tg * 512:(tg + 1) * 512],
                                  wthi[:, s][:, icx, :],
                                  xhi[:, icx, tg * 512:(tg + 1) * 512],
                                  start=(icx == 0), stop=False)
                    for tg in range(4):
                        te.matmul(pw[:, tg * 512:(tg + 1) * 512],
                                  wthi[:, s][:, icx, :],
                                  xlo[:, icx, tg * 512:(tg + 1) * 512],
                                  start=False, stop=False)
                    te.matmul(pcs[:, 0:1], wthi[:, s][:, icx, :], negcbf[:],
                              start=(icx == 0), stop=False)
                    for tg in range(4):
                        te.matmul(pw[:, tg * 512:(tg + 1) * 512],
                                  wtlo[:, s][:, icx, :],
                                  xhi[:, icx, tg * 512:(tg + 1) * 512],
                                  start=False, stop=(icx == IC - 1))
                    mm = te.matmul(pcs[:, 0:1], wtlo[:, s][:, icx, :], negcbf[:],
                                   start=False, stop=(icx == IC - 1))
                inc(mm, sPEb, 1, mark=f"p1.proj.{h}")
                if h >= 1:
                    trb(te, h - 1)
            trb(te, H - 1)

        def p1_ve(ve):
            def epi(h):
                w(ve, sPEb, f"p1.proj.{h}")
                if h >= 2:
                    w(ve, sWST, f"p1.wst.{h - 2}")
                    w(ve, sPEa, f"p1.tr.{h - 2}.15")
                vguard(ve)
                ve.tensor_copy(csn[:], pcs[:, 0:1])
                inc(ve.tensor_tensor(w_h[:, h % 2, :], pw[:, :], rbc[:],
                                     ALU.mult),
                    sVEa, 1, mark=f"p1.ep1.{h}")
                inc(ve.scalar_tensor_tensor(w_h[:, h % 2, :], mrbc[:], csn[:, 0:1],
                                            w_h[:, h % 2, :], ALU.mult, ALU.add),
                    sVEa, 1, mark=f"p1.wh.{h}")

            def trev(h):
                if h >= 1:
                    w(ve, sWTT, f"p1.wtt.{h - 1}")
                for tc in range(16):
                    w(ve, sPEa, f"p1.tr.{h}.{tc}")
                    vguard(ve)
                    ptw = [ptw0, ptw1, ptw2]
                    i1 = ve.tensor_copy(wts[:, tc, :], ptw[tc % 3][:, 0:128])
                    inc(i1, sVEa, 1, mark=f"p1.wtev.{h}.{tc}")
                mark(f"p1.wtse.{h}", sVEa)

            for h in range(H):
                s2 = h % 2
                w(ve, sWTF[s2], f"p1.wtf.{h}")
                w(ve, sWTH[s2], f"p1.wthi.{h}")
                if h >= 2:
                    w(ve, sPEb, f"p1.proj.{h - 2}")
                inc(ve.tensor_tensor(wtlo[:, s2], wtf[:, s2], wthi[:, s2],
                                     ALU.subtract), sVEa, 1,
                    mark=f"p1.wtlo.{h}")
                if h >= 1:
                    epi(h - 1)
                if h >= 2:
                    trev(h - 2)
            epi(H - 1)
            trev(H - 2)
            trev(H - 1)

        P.run_phase({"sync": p1_sync, "gpsimd": p1_gp, "tensor": p1_te,
                     "vector": p1_ve})

    with ExitStack() as ph:
        wgf = ph.enter_context(nc.sbuf_tensor("wgf", [128, 2, IC, 128], F32))
        sgnc = ph.enter_context(nc.sbuf_tensor("sgnc", [128, 2, IC, 128], BF16))
        absc = ph.enter_context(nc.sbuf_tensor("absc", [128, 8, 128], F32))
        negcs = ph.enter_context(nc.sbuf_tensor("negcs", [128, 16], F32))
        ainv = ph.enter_context(nc.sbuf_tensor("ainv", [128, 16], F32))
        ta = ph.enter_context(nc.sbuf_tensor("ta", [128, TH], F32))
        tb = ph.enter_context(nc.sbuf_tensor("tb", [128, TH], F32))
        rT = ph.enter_context(nc.sbuf_tensor("rT", [128, 2, TH], F32))
        pS = ph.enter_context(nc.psum_tensor("pS", [128, 2, TH], F32))
        pal = ph.enter_context(nc.psum_tensor("pal", [128, 512], F32))
        pcb = ph.enter_context(nc.psum_tensor("pcb", [128, 512], F32))

        def p15_sync(sy):
            for oc in range(16):
                s = oc % 2
                if oc == 0:
                    w(sy, sVEa, "p1.wh.15")
                    w(sy, sWST, "p1.wst.15")
                    w(sy, sWTF[0], "p1.wtf.14")
                    w(sy, sWTF[1], "p1.wtf.15")
                    w(sy, sWTH[0], "p1.wthi.14")
                    w(sy, sWTH[1], "p1.wthi.15")
                if oc >= 2:
                    w(sy, sVEa, f"p15.sgn.{oc - 2}")
                    w(sy, sACa, f"p15.abs.{oc - 2}")
                inc(sy.dma_start(out=wgf[:, s],
                                 in_=wgtTv[:, :, oc * 128:(oc + 1) * 128]),
                    sWGF[s], 16, mark=f"p15.wgf.{oc}")
                if oc >= 2:
                    w(sy, sVEa, f"p15.rT.{oc - 2}")
                    inc(sy.dma_start(out=resFv[:, oc - 2, :], in_=rT[:, (oc - 2) % 2]),
                        sRST[oc % 2], 16, mark=f"p15.rst.{oc - 2}")
            for oc in (14, 15):
                w(sy, sVEa, f"p15.rT.{oc}")
                inc(sy.dma_start(out=resFv[:, oc, :], in_=rT[:, oc % 2]),
                    sRST[oc % 2], 16, mark=f"p15.rst.{oc}")

        def p15_sc(sc):
            k = 0
            for oc in range(16):
                s = oc % 2
                w(sc, sWGF[s], f"p15.wgf.{oc}")
                for icx in range(IC):
                    if k >= 8:
                        w(sc, sPEa, f"p15.amm.{k - 8}")
                    inc(sc.activation(absc[:, k % 8], wgf[:, s][:, icx, :], AF.Abs),
                        sACa, 1, mark=f"p15.absk.{k}")
                    k += 1
                mark(f"p15.abs.{oc}", sACa)

        def p15_ve(ve):
            for oc in range(16):
                s = oc % 2
                w(ve, sWGF[s], f"p15.wgf.{oc}")
                if oc >= 2:
                    w(ve, sPEb, f"p15.S.{oc - 2}")
                inc(ve.tensor_scalar(sgnc[:, s], wgf[:, s], 0.0, 2.0,
                                     ALU.is_ge, ALU.mult), sVEa, 1,
                    mark=f"p15.sgn0.{oc}")
                w(ve, sVEa, f"p15.sgn0.{oc}")
                inc(ve.tensor_scalar_add(sgnc[:, s], sgnc[:, s], -1.0), sVEa, 1,
                    mark=f"p15.sgn.{oc}")
                w(ve, sPEb, f"p15.acs.{oc}")
                vguard(ve)
                ve.tensor_scalar_mul(ainv[:, oc:oc + 1], pal[:, 0:1], INV_D)
                inc(ve.tensor_copy(negcs[:, oc:oc + 1], pcb[:, 0:1]), sVEa, 1,
                    mark=f"p15.ae.{oc}")
                w(ve, sPEb, f"p15.S.{oc}")
                if oc >= 2:
                    w(ve, sRST[oc % 2], f"p15.rst.{oc - 2}")
                vguard(ve)
                ve.tensor_tensor(ta[:], pS[:, s, :], rbc[:, 0:TH], ALU.mult)
                ve.scalar_tensor_tensor(tb[:], mrbc[:, 0:TH], negcs[:, oc:oc + 1],
                                        ta[:], ALU.mult, ALU.add)
                inc(ve.tensor_scalar(rT[:, s][:, :], tb[:], ainv[:, oc:oc + 1],
                                     biast[:, oc:oc + 1], ALU.mult, ALU.add),
                    sVEa, 1, mark=f"p15.rT.{oc}")

        def p15_te(te):
            for oc in range(16):
                s = oc % 2
                if oc >= 1:
                    w(te, sVEa, f"p15.ae.{oc - 1}")
                mm = None
                for icx in range(IC):
                    w(te, sACa, f"p15.absk.{oc * IC + icx}")
                    mm = te.matmul(pal[:, 0:1], absc[:, (oc * IC + icx) % 8],
                                   onesc[:], start=(icx == 0),
                                   stop=(icx == IC - 1))
                    inc(mm, sPEa, 1, mark=f"p15.amm.{oc * IC + icx}")
                w(te, sVEa, f"p15.sgn.{oc}")
                for icx in range(IC):
                    mm = te.matmul(pcb[:, 0:1], sgnc[:, s][:, icx, :], negcbf[:],
                                   start=(icx == 0), stop=(icx == IC - 1))
                inc(mm, sPEb, 1, mark=f"p15.acs.{oc}")
                if oc >= 2:
                    w(te, sVEa, f"p15.rT.{oc - 2}")
                mm = None
                for icx in range(IC):
                    for q2 in range(2):
                        te.matmul(pS[:, s, q2 * 512:(q2 + 1) * 512],
                                  sgnc[:, s][:, icx, :],
                                  xhi[:, icx, q2 * 512:(q2 + 1) * 512],
                                  start=(icx == 0), stop=False)
                    for q2 in range(2):
                        mm = te.matmul(pS[:, s, q2 * 512:(q2 + 1) * 512],
                                       sgnc[:, s][:, icx, :],
                                       xlo[:, icx, q2 * 512:(q2 + 1) * 512],
                                       start=False, stop=(icx == IC - 1))
                inc(mm, sPEb, 1, mark=f"p15.S.{oc}")

        P.run_phase({"sync": p15_sync, "scalar": p15_sc, "vector": p15_ve,
                     "tensor": p15_te})

    mid.close()

    with ExitStack() as ph:
        w_h2 = ph.enter_context(nc.sbuf_tensor("w_h2", [128, 2, T], F32))
        sel = ph.enter_context(nc.sbuf_tensor("sel", [16, 2048], F32))
        wtok = ph.enter_context(nc.sbuf_tensor("wtok", [128, 2, 16, 128], F32))
        vf = ph.enter_context(nc.sbuf_tensor("vf", [128, 16, 128], F32))
        khi = ph.enter_context(nc.sbuf_tensor("khi", [128, 2, T], BF16))
        klo = ph.enter_context(nc.sbuf_tensor("klo", [128, 2, T], BF16))
        qf = ph.enter_context(nc.sbuf_tensor("qf", [128, T], F32))
        qhi = ph.enter_context(nc.sbuf_tensor("qhi", [128, T], BF16))
        qlo = ph.enter_context(nc.sbuf_tensor("qlo", [128, T], BF16))
        vhi = ph.enter_context(nc.sbuf_tensor("vhi", [128, 16, 128], BF16))
        vlo = ph.enter_context(nc.sbuf_tensor("vlo", [128, 16, 128], BF16))
        vphi = ph.enter_context(nc.sbuf_tensor("vphi", [128, 16, 128], BF16))
        vplo = ph.enter_context(nc.sbuf_tensor("vplo", [128, 16, 128], BF16))
        ehi = ph.enter_context(nc.sbuf_tensor("ehi", [128, 16, TH], BF16))
        elo = ph.enter_context(nc.sbuf_tensor("elo", [128, 16, TH], BF16))
        ef32 = ph.enter_context(nc.sbuf_tensor("ef32", [128, 4, TH], F32))
        dump = ph.enter_context(nc.sbuf_tensor("dump", [128, TH], F32))
        ms = ph.enter_context(nc.sbuf_tensor("ms", [128, 2, 16], F32))
        msr = ph.enter_context(nc.sbuf_tensor("msr", [128, 2, 16], F32))
        mst = ph.enter_context(nc.sbuf_tensor("mst", [128, 2, 16], F32))
        rsqh = ph.enter_context(nc.sbuf_tensor("rsqh", [128, 2, 16], F32))
        rsqTs = ph.enter_context(nc.sbuf_tensor("rsqTs", [16, 128], F32))
        zm = ph.enter_context(nc.sbuf_tensor("zm", [128, 16], F32))
        zo = ph.enter_context(nc.sbuf_tensor("zo", [128, 16], F32))
        zinv = ph.enter_context(nc.sbuf_tensor("zinv", [128, 16], F32))
        zTs = ph.enter_context(nc.sbuf_tensor("zTs", [16, 128], F32))
        zbc = ph.enter_context(nc.sbuf_tensor("zbc", [128, TH], F32))
        yt = ph.enter_context(nc.sbuf_tensor("yt", [128, TH], F32))
        yf = ph.enter_context(nc.sbuf_tensor("yf", [128, TH], F32))
        yhib = ph.enter_context(nc.sbuf_tensor("yhib", [128, TH], BF16))
        ylob = ph.enter_context(nc.sbuf_tensor("ylob", [128, TH], BF16))
        msdump = ph.enter_context(nc.sbuf_tensor("msdump", [128, 128], F32))
        patt0 = ph.enter_context(nc.psum_tensor("patt0", [128, TH], F32))
        patt1 = ph.enter_context(nc.psum_tensor("patt1", [128, TH], F32))
        patt2 = ph.enter_context(nc.psum_tensor("patt2", [128, TH], F32))
        patt3 = ph.enter_context(nc.psum_tensor("patt3", [128, TH], F32))
        patt = [patt0, patt1, patt2, patt3]

        def p2_sync(sy):
            def ystores(h):
                w(sy, sVEa, f"p2.yc.{h}")
                inc(sy.dma_start(out=yhsc[h], in_=yhib[:]), sYST, 16)
                inc(sy.dma_start(out=ylsc[h], in_=ylob[:]), sYST, 16,
                    mark=f"p2.yst.{h}")

            for h in range(H):
                s = h % 2
                if h == 0:
                    w(sy, sVEa, "p15.rT.15")
                    inc(sy.dma_start(out=sel[:], in_=seld), sCONST, 16,
                        mark="p2.selld")
                    w(sy, sWGF[0], "p15.wgf.14")
                    w(sy, sWGF[1], "p15.wgf.15")
                    w(sy, sRST[0], "p15.rst.14")
                    w(sy, sRST[1], "p15.rst.15")
                if h >= 2:
                    w(sy, sVEa, f"p2.whfree.{h - 2}")
                w(sy, sWST, f"p1.wst.{h}")
                inc(sy.dma_start(out=w_h2[:, s, :], in_=wsc[h]), sWLD[s], 16,
                    mark=f"p2.wld.{h}")
                if h >= 2:
                    w(sy, sACa, f"p2.ms.{h - 2}.15")
                    w(sy, sVEa, f"p2.vstt.{h - 2}")
                w(sy, sWTT, f"p1.wtt.{h}")
                inc(sy.dma_start(out=wtok[:, s], in_=wscT[h]), sWTK[s], 16,
                    mark=f"p2.wtk.{h}")
                if dbg and h == 0:
                    w(sy, sVEa, f"p2.rms1.{h}")
                    inc(sy.dma_start(out=dbg_ms, in_=ms[:, :]), sDBG, 16)
                    inc(sy.dma_start(out=dbg_msr, in_=msr[:, :]), sDBG, 16)
                    inc(sy.dma_start(out=dbg_mst, in_=mst[:, :]), sDBG, 16,
                        mark="dbg.mst")
                    w(sy, sACa, f"p2.sqrt.{h}")
                    inc(sy.dma_start(out=dbg_sq0, in_=rsqh[:, :]), sDBG, 16,
                        mark="dbg.sq0")
                    w(sy, sVEa, f"p2.rsq.{h}")
                    inc(sy.dma_start(out=dbg_rsq, in_=rsqh[:, :]), sDBG, 16,
                        mark="dbg.rsq")
                    w(sy, sVEa, f"p2.qk.{h}")
                    inc(sy.dma_start(out=dbg_qf, in_=qf[:, :]), sDBG, 16)
                    inc(sy.dma_start(out=dbg_khi, in_=khi[:, :]), sDBG, 16)
                    inc(sy.dma_start(out=dbg_wtok, in_=wtok[:, :, :]), sDBG, 16)
                    w(sy, sVEa, f"p2.v.{h}")
                    inc(sy.dma_start(out=dbg_vf, in_=vf[:, :, :]), sDBG, 16)
                    w(sy, sVEa, f"p2.es.{h}.15")
                    inc(sy.dma_start(out=dbg_ehi, in_=ehi[:, :, :]), sDBG, 16)
                    inc(sy.dma_start(out=dbg_elo, in_=elo[:, :, :]), sDBG, 16)
                    w(sy, sVEb, f"p2.z.{h}")
                    inc(sy.dma_start(out=dbg_z, in_=zm[:, :]), sDBG, 16)
                    inc(sy.dma_start(out=dbg_zi, in_=zinv[:, :]), sDBG, 16)
                    w(sy, sVEa, f"p2.zbce.{h}")
                    inc(sy.dma_start(out=dbg_zbc, in_=zbc[:, :]), sDBG, 16)
                if h >= 1:
                    ystores(h - 1)
            ystores(H - 1)

        def p2_te(te):
            for h in range(H):
                s = h % 2
                w(te, sWLD[s], f"p2.wld.{h}")
                w(te, sVEa, f"p2.rsq.{h}")
                if h >= 1:
                    w(te, sVEa, f"p2.zTe.{h - 1}")
                tt = te.transpose(patt2[0:16, 0:128], rsqh[:, h % 2, :], ident[:])
                inc(tt, sPEa, 1, mark=f"p2.rsqT.{h}")
                if h == 0:
                    w(te, sCONST, "p2.selld")
                if h >= 1:
                    w(te, sVEa, f"p2.zbce.{h - 1}")
                w(te, sVEa, f"p2.rsqTe.{h}")
                for tc in range(16):
                    if tc >= 2:
                        w(te, sVEa, f"p2.pbcf.{h}.{tc - 2}")
                    mm = te.matmul(patt[2 + tc % 2][:, 512:640],
                                   sel[:, tc * 128:(tc + 1) * 128],
                                   rsqTs[:, :], start=True, stop=True)
                    inc(mm, sPEa, 1, mark=f"p2.sel.{h}.{tc}")
                w(te, sVEa, f"p2.qk.{h}")
                if h >= 1:
                    w(te, sVEa, f"p2.yc.{h - 1}")
                for jc in range(16):
                    for ih in range(2):
                        kstep = jc * 2 + ih
                        if kstep >= 4:
                            w(te, sACb, f"p2.exp.{h}.{kstep - 4}")
                        mm = None
                        for q2 in range(2):
                            mm = te.matmul(
                                patt[kstep % 4][:, q2 * 512:(q2 + 1) * 512],
                                khi[:, h % 2, jc * 128:(jc + 1) * 128],
                                qhi[:, ih * TH + q2 * 512: ih * TH + (q2 + 1) * 512],
                                start=True, stop=False)
                        for q2 in range(2):
                            mm = te.matmul(
                                patt[kstep % 4][:, q2 * 512:(q2 + 1) * 512],
                                khi[:, h % 2, jc * 128:(jc + 1) * 128],
                                qlo[:, ih * TH + q2 * 512: ih * TH + (q2 + 1) * 512],
                                start=False, stop=False)
                        for q2 in range(2):
                            mm = te.matmul(
                                patt[kstep % 4][:, q2 * 512:(q2 + 1) * 512],
                                klo[:, h % 2, jc * 128:(jc + 1) * 128],
                                qhi[:, ih * TH + q2 * 512: ih * TH + (q2 + 1) * 512],
                                start=False, stop=(True))
                        inc(mm, sPEb, 1, mark=f"p2.att.{h}.{kstep}")
                w(te, sACb, f"p2.exp.{h}.28")
                mm = None
                for jc in range(16):
                    w(te, sVEa, f"p2.es.{h}.{jc}")
                    for q2 in range(2):
                        te.matmul(patt0[:, q2 * 512:(q2 + 1) * 512],
                                  vhi[:, jc, :],
                                  ehi[:, jc, q2 * 512:(q2 + 1) * 512],
                                  start=(jc == 0), stop=False)
                    for q2 in range(2):
                        te.matmul(patt0[:, q2 * 512:(q2 + 1) * 512],
                                  vhi[:, jc, :],
                                  elo[:, jc, q2 * 512:(q2 + 1) * 512],
                                  start=False, stop=False)
                    for q2 in range(2):
                        mm = te.matmul(patt0[:, q2 * 512:(q2 + 1) * 512],
                                       vlo[:, jc, :],
                                       ehi[:, jc, q2 * 512:(q2 + 1) * 512],
                                       start=False, stop=(jc == 15))
                inc(mm, sPEb, 1, mark=f"p2.y1.{h}")
                w(te, sVEa, f"p2.vp.{h}")
                w(te, sACb, f"p2.exp.{h}.29")
                mm = None
                for jc in range(16):
                    for q2 in range(2):
                        te.matmul(patt1[:, q2 * 512:(q2 + 1) * 512],
                                  vphi[:, jc, :],
                                  ehi[:, jc, q2 * 512:(q2 + 1) * 512],
                                  start=(jc == 0), stop=False)
                    for q2 in range(2):
                        te.matmul(patt1[:, q2 * 512:(q2 + 1) * 512],
                                  vphi[:, jc, :],
                                  elo[:, jc, q2 * 512:(q2 + 1) * 512],
                                  start=False, stop=False)
                    for q2 in range(2):
                        mm = te.matmul(patt1[:, q2 * 512:(q2 + 1) * 512],
                                       vplo[:, jc, :],
                                       ehi[:, jc, q2 * 512:(q2 + 1) * 512],
                                       start=False, stop=(jc == 15))
                inc(mm, sPEb, 1, mark=f"p2.y.{h}")
                w(te, sVEb, f"p2.z.{h}")
                w(te, sACb, f"p2.exp.{h}.31")
                tt = te.transpose(patt2[0:16, 0:128], zinv[:, :], ident[:])
                inc(tt, sPEa, 1, mark=f"p2.zT.{h}")
                w(te, sVEa, f"p2.zTe.{h}")
                for tc in range(8):
                    if tc >= 2:
                        w(te, sVEa, f"p2.zbcev.{h}.{tc - 2}")
                    mm = te.matmul(patt[2 + tc % 2][:, 640:768],
                                   sel[:, tc * 128:(tc + 1) * 128],
                                   zTs[:, :], start=True, stop=True)
                    inc(mm, sPEa, 1, mark=f"p2.zsel.{h}.{tc}")

        def p2_sc(sc):
            def mschain(h):
                s2 = h % 2
                w(sc, sWTK[s2], f"p2.wtk.{h}")
                for tc in range(16):
                    inc(sc.activation(msdump[:], wtok[:, s2, tc, :], AF.Square,
                                      accum_out=ms[:, s2, tc:tc + 1]),
                        sACa, 1, mark=f"p2.ms.{h}.{tc}")
                w(sc, sVEa, f"p2.rms1.{h}")
                if dbg and h == 0:
                    w(sc, sDBG, "dbg.mst")
                inc(sc.activation(rsqh[:, s2, :], mst[:, s2, :], AF.Sqrt),
                    sACa, 1, mark=f"p2.sqrt.{h}")

            for h in range(H):
                s2 = h % 2
                if h == 0:
                    mschain(0)
                if h >= 1:
                    w(sc, sVEb, f"p2.z.{h - 1}")
                for jc in range(16):
                    for ih in range(2):
                        kstep = jc * 2 + ih
                        w(sc, sPEb, f"p2.att.{h}.{kstep}")
                        aguard(sc)
                        if ih == 0:
                            if jc >= 4:
                                w(sc, sVEa, f"p2.es.{h}.{jc - 4}")
                            i1 = sc.activation(ef32[:, jc % 4], patt[kstep % 4][:, :],
                                               AF.Exp, bias=cshift[:, 0:1],
                                               scale=rsqh[:, s2, jc:jc + 1],
                                               accum_out=zm[:, jc:jc + 1])
                        else:
                            i1 = sc.activation(dump[:], patt[kstep % 4][:, :],
                                               AF.Exp, bias=cshift[:, 0:1],
                                               scale=rsqh[:, s2, jc:jc + 1],
                                               accum_out=zo[:, jc:jc + 1])
                        inc(i1, sACb, 1, mark=f"p2.exp.{h}.{kstep}")
                if h + 1 < H:
                    mschain(h + 1)

        def p2_ve(ve):
            def rmschain(h):
                s2 = h % 2
                w(ve, sACa, f"p2.ms.{h}.15")
                inc(ve.tensor_scalar(msr[:, s2, :], ms[:, s2, :], INV_DH, EPS_RMS,
                                     ALU.mult, ALU.add), sVEa, 1,
                    mark=f"p2.msr.{h}")
                w(ve, sVEa, f"p2.msr.{h}")
                inc(ve.reciprocal(mst[:, s2, :], msr[:, s2, :]), sVEa, 1,
                    mark=f"p2.rms1.{h}")
                w(ve, sACa, f"p2.sqrt.{h}")
                if dbg and h == 0:
                    w(ve, sDBG, "dbg.sq0")
                for it in range(2):
                    inc(ve.tensor_mul(mst[:, s2, :], rsqh[:, s2, :], rsqh[:, s2, :]),
                        sVEa, 1, mark=f"p2.n{it}a.{h}")
                    w(ve, sVEa, f"p2.n{it}a.{h}")
                    inc(ve.tensor_mul(mst[:, s2, :], mst[:, s2, :], msr[:, s2, :]),
                        sVEa, 1, mark=f"p2.n{it}b.{h}")
                    w(ve, sVEa, f"p2.n{it}b.{h}")
                    inc(ve.tensor_scalar(mst[:, s2, :], mst[:, s2, :], -0.5, 1.5,
                                         ALU.mult, ALU.add), sVEa, 1,
                        mark=f"p2.n{it}c.{h}")
                    w(ve, sVEa, f"p2.n{it}c.{h}")
                    inc(ve.tensor_mul(rsqh[:, s2, :], rsqh[:, s2, :], mst[:, s2, :]),
                        sVEa, 1, mark=f"p2.n{it}d.{h}")
                    w(ve, sVEa, f"p2.n{it}d.{h}")
                inc(ve.tensor_copy(mst[:, s2, :], rsqh[:, s2, :]), sVEa, 1,
                    mark=f"p2.rsq.{h}")

            def kcasts(h):
                s2 = h % 2
                w(ve, sWLD[s2], f"p2.wld.{h}")
                ve.tensor_copy(khi[:, s2], w_h2[:, s2, :])
                i1 = ve.tensor_tensor(klo[:, s2], w_h2[:, s2, :], khi[:, s2],
                                      ALU.subtract)
                inc(i1, sVEa, 1, mark=f"p2.kc.{h}")

            for h in range(H):
                s = h % 2
                if h == 0:
                    rmschain(0)
                    kcasts(0)
                w(ve, sPEa, f"p2.rsqT.{h}")
                vguard(ve)
                inc(ve.tensor_copy(rsqTs[:, :], patt2[0:16, 0:128]), sVEa, 1,
                    mark=f"p2.rsqTe.{h}")
                for tc in range(16):
                    w(ve, sPEa, f"p2.sel.{h}.{tc}")
                    vguard(ve)
                    i1 = ve.scalar_tensor_tensor(qf[:, tc * 128:(tc + 1) * 128],
                                                 w_h2[:, s, tc * 128:(tc + 1) * 128],
                                                 gg[:, 0:1],
                                                 patt[2 + tc % 2][:, 512:640],
                                                 ALU.mult, ALU.mult)
                    inc(i1, sVEa, 1, mark=f"p2.pbcf.{h}.{tc}")
                w(ve, sVEa, f"p2.pbcf.{h}.15")
                ve.tensor_copy(qhi[:], qf[:])
                i1 = ve.tensor_tensor(qlo[:], qf[:], qhi[:], ALU.subtract)
                inc(i1, sVEa, 2)
                mark(f"p2.qk.{h}", sVEa)
                mark(f"p2.whfree.{h}", sVEa)
                for tc in range(16):
                    i1 = ve.scalar_tensor_tensor(vf[:, tc, :],
                                                 wtok[:, s, tc, :],
                                                 rsqh[:, s, tc:tc + 1], g3bc[:],
                                                 ALU.mult, ALU.mult)
                inc(i1, sVEa, 1, mark=f"p2.vstt.{h}")
                w(ve, sVEa, f"p2.vstt.{h}")
                ve.tensor_copy(vhi[:], vf[:])
                inc(ve.tensor_tensor(vlo[:], vf[:], vhi[:], ALU.subtract),
                    sVEa, 1, mark=f"p2.v.{h}")
                for jc in range(16):
                    w(ve, sACb, f"p2.exp.{h}.{jc * 2}")
                    ve.tensor_copy(ehi[:, jc, :], ef32[:, jc % 4])
                    i1 = ve.tensor_tensor(elo[:, jc, :], ef32[:, jc % 4],
                                          ehi[:, jc, :], ALU.subtract)
                    inc(i1, sVEa, 1, mark=f"p2.es.{h}.{jc}")
                w(ve, sACb, f"p2.exp.{h}.31")
                inc(ve.tensor_add(zm[:, :], zm[:, :], zo[:, :]), sVEa, 1,
                    mark=f"p2.zsum.{h}")
                w(ve, sVEa, f"p2.zsum.{h}")
                inc(ve.reciprocal(zinv[:, :], zm[:, :]), sVEb, 1,
                    mark=f"p2.z.{h}")
                w(ve, sVEb, f"p2.z.{h}")
                for tc in range(16):
                    ve.tensor_single_scalar(vphi[:, tc, :], vf[:, tc, :],
                                            zinv[:, tc:tc + 1], ALU.mult)
                    ve.scalar_tensor_tensor(vplo[:, tc, :], vf[:, tc, :],
                                            zinv[:, tc:tc + 1], vphi[:, tc, :],
                                            ALU.mult, ALU.subtract)
                i1 = ve.tensor_copy(msdump[:, 0:1], zinv[:, 0:1])
                inc(i1, sVEa, 1, mark=f"p2.vp.{h}")
                if h + 1 < H:
                    rmschain(h + 1)
                    kcasts(h + 1)
                w(ve, sPEa, f"p2.zT.{h}")
                vguard(ve)
                inc(ve.tensor_copy(zTs[:, :], patt2[0:16, 0:128]), sVEa, 1,
                    mark=f"p2.zTe.{h}")
                for tc in range(8):
                    w(ve, sPEa, f"p2.zsel.{h}.{tc}")
                    vguard(ve)
                    i1 = ve.tensor_copy(zbc[:, tc * 128:(tc + 1) * 128],
                                        patt[2 + tc % 2][:, 640:768])
                    inc(i1, sVEa, 1, mark=f"p2.zbcev.{h}.{tc}")
                mark(f"p2.zbce.{h}", sVEa)
                w(ve, sPEb, f"p2.y.{h}")
                if h >= 1:
                    w(ve, sYST, f"p2.yst.{h - 1}")
                vguard(ve)
                ve.tensor_tensor(yt[:], patt0[:, :], zbc[:], ALU.mult)
                inc(ve.tensor_add(yf[:], yt[:], patt1[:, :]), sVEa, 1,
                    mark=f"p2.yf.{h}")
                w(ve, sVEa, f"p2.yf.{h}")
                ve.tensor_copy(yhib[:], yf[:])
                i1 = ve.tensor_tensor(ylob[:], yf[:], yhib[:], ALU.subtract)
                inc(i1, sVEa, 1, mark=f"p2.yc.{h}")

        P.run_phase({"sync": p2_sync, "tensor": p2_te, "scalar": p2_sc,
                     "vector": p2_ve})

    with ExitStack() as ph:
        ybh = ph.enter_context(nc.sbuf_tensor("ybh", [128, H, TH], BF16))
        ybl = ph.enter_context(nc.sbuf_tensor("ybl", [128, H, TH], BF16))
        wf3 = ph.enter_context(nc.sbuf_tensor("wf3", [128, 2, TH], F32))
        whih = ph.enter_context(nc.sbuf_tensor("whih", [128, 16, TH], BF16))
        wloh = ph.enter_context(nc.sbuf_tensor("wloh", [128, 16, TH], BF16))
        rfb = ph.enter_context(nc.sbuf_tensor("rfb", [128, 2, TH], F32))
        resb = ph.enter_context(nc.sbuf_tensor("resb", [128, 2, TH], F32))
        xhb = ph.enter_context(nc.sbuf_tensor("xhb", [128, 2, TH], F32))
        outb = ph.enter_context(nc.sbuf_tensor("outb", [128, 2, TH], F32))
        attcp = ph.enter_context(nc.sbuf_tensor("attcp", [128, TH], F32)) if dbg else None
        pout0 = ph.enter_context(nc.psum_tensor("pout0", [128, TH], F32))
        pout1 = ph.enter_context(nc.psum_tensor("pout1", [128, TH], F32))
        ptr3A = ph.enter_context(nc.psum_tensor("ptr3A", [128, 512], F32))
        ptr3B = ph.enter_context(nc.psum_tensor("ptr3B", [128, 512], F32))
        pout = [pout0, pout1]
        ptr3 = [ptr3A, ptr3B]

        def p3_sync(sy):
            w(sy, sPEb, f"p2.y.{H - 1}")
            w(sy, sWLD[0], "p2.wld.14")
            w(sy, sWLD[1], "p2.wld.15")
            for h in range(H):
                w(sy, sYST, f"p2.yst.{h}")
                inc(sy.dma_start(out=ybh[:, h, :], in_=yhsc[h]), sP3Y, 16)
                inc(sy.dma_start(out=ybl[:, h, :], in_=ylsc[h]), sP3Y, 16,
                    mark=f"p3.yld.{h}")
            for ch in range(2):
                for fc in range(16):
                    s = fc % 2
                    if fc >= 2 or ch > 0:
                        prev = (f"p3.wlo.{ch}.{fc - 2}" if fc >= 2 else
                                f"p3.wlo.{ch - 1}.{14 + fc}")
                        w(sy, sVEa, prev)
                    inc(sy.dma_start(out=wf3[:, s],
                                     in_=Wnv[:, fc, ch * TH:(ch + 1) * TH]),
                        sWF[s], 16, mark=f"p3.wf.{ch}.{fc}")
                for tc in range(8):
                    s = tc % 2
                    if tc >= 2 or ch > 0:
                        prev = (f"p3.oe.{ch}.{tc - 2}" if tc >= 2 else
                                f"p3.oe.{ch - 1}.{6 + tc}")
                        w(sy, sVEa, prev)
                    for q8 in range(8):
                        inc(sy.dma_start(
                            out=rfb[:, s][:, q8 * 128:(q8 + 1) * 128],
                            in_=resFv[:, ch * 8 + q8, tc * 128:(tc + 1) * 128]),
                            sRX[s], 16)
                    inc(sy.dma_start(out=xhb[:, s],
                                     in_=xhv[:, tc, ch * TH:(ch + 1) * TH]),
                        sRX[s], 16, mark=f"p3.rx.{ch}.{tc}")
                    w(sy, sVEa, f"p3.oe.{ch}.{tc}")
                    if dbg and ch == 1 and tc == 0:
                        inc(sy.dma_start(out=dbg_att, in_=attcp[:]), sDBG, 16)
                        inc(sy.dma_start(out=dbg_resb, in_=resb[:, 0]), sDBG, 16)
                        inc(sy.dma_start(out=dbg_xhb2, in_=xhb[:, 0]), sDBG, 16,
                            mark="dbg.p3")
                    inc(sy.dma_start(out=outv[:, tc, ch * TH:(ch + 1) * TH],
                                     in_=outb[:, s]), sOST[s], 16,
                        mark=f"p3.ost.{ch}.{tc}")
            w(sy, sOST[0], "p3.ost.1.6")
            w(sy, sOST[1], "p3.ost.1.7")

        def p3_gp(g):
            for ch in range(2):
                if ch == 0:
                    w(g, sPEb, f"p2.y.{H - 1}")
                    w(g, sWLD[0], "p2.wld.14")
                    w(g, sWLD[1], "p2.wld.15")
                    w(g, sWTH[0], "p1.wthi.14")
                    w(g, sWTH[1], "p1.wthi.15")
                if ch > 0:
                    w(g, sPEb, f"p3.mm.{ch - 1}.7")
                for fc in range(16):
                    inc(g.dma_start(out=whih[:, fc, :],
                                    in_=Wnv[:, fc, ch * TH:(ch + 1) * TH]),
                        sWHIF[fc], 16, mark=f"p3.whi.{ch}.{fc}")

        def p3_te(te):
            for ch in range(2):
                for tc in range(8):
                    sp = tc % 2
                    if tc >= 2 or ch > 0:
                        prev = (f"p3.oe.{ch}.{tc - 2}" if tc >= 2 else
                                f"p3.oe.{ch - 1}.{6 + tc}")
                        w(te, sVEa, prev)
                    for h in range(H):
                        w(te, sVEa, f"p3.wlo.{ch}.{h}")
                        w(te, sP3Y, f"p3.yld.{h}")
                        for q2 in range(2):
                            te.matmul(pout[sp][:, q2 * 512:(q2 + 1) * 512],
                                      ybh[:, h, tc * 128:(tc + 1) * 128],
                                      whih[:, h, q2 * 512:(q2 + 1) * 512],
                                      start=(h == 0), stop=False)
                        for q2 in range(2):
                            te.matmul(pout[sp][:, q2 * 512:(q2 + 1) * 512],
                                      ybh[:, h, tc * 128:(tc + 1) * 128],
                                      wloh[:, h, q2 * 512:(q2 + 1) * 512],
                                      start=False, stop=False)
                        for q2 in range(2):
                            mm = te.matmul(pout[sp][:, q2 * 512:(q2 + 1) * 512],
                                           ybl[:, h, tc * 128:(tc + 1) * 128],
                                           whih[:, h, q2 * 512:(q2 + 1) * 512],
                                           start=False, stop=(h == H - 1))
                    inc(mm, sPEb, 1, mark=f"p3.mm.{ch}.{tc}")
                    w(te, sRX[tc % 2], f"p3.rx.{ch}.{tc}")
                    for q8 in range(8):
                        if q8 >= 2:
                            w(te, sVEa, f"p3.rtev.{ch}.{tc}.{q8 - 2}")
                        elif tc >= 1 or ch > 0:
                            prev = (f"p3.rtev.{ch}.{tc - 1}.{6 + q8}" if tc >= 1
                                    else f"p3.rtev.{ch - 1}.{7}.{6 + q8}")
                            w(te, sVEa, prev)
                        tt = te.transpose(ptr3[q8 % 2][:, 0:128],
                                          rfb[:, tc % 2][:, q8 * 128:(q8 + 1) * 128],
                                          ident[:])
                        inc(tt, sPEa, 1, mark=f"p3.rt.{ch}.{tc}.{q8}")

        def p3_ve(ve):
            for ch in range(2):
                if ch > 0:
                    w(ve, sPEb, f"p3.mm.{ch - 1}.7")
                for fc in range(16):
                    s = fc % 2
                    w(ve, sWF[s], f"p3.wf.{ch}.{fc}")
                    w(ve, sWHIF[fc], f"p3.whi.{ch}.{fc}")
                    i1 = ve.tensor_tensor(wloh[:, fc, :], wf3[:, s],
                                          whih[:, fc, :], ALU.subtract)
                    inc(i1, sVEa, 1, mark=f"p3.wlo.{ch}.{fc}")
                for tc in range(8):
                    sp = tc % 2
                    if dbg and ch == 1 and tc == 2:
                        w(ve, sDBG, "dbg.p3")
                    for q8 in range(8):
                        w(ve, sPEa, f"p3.rt.{ch}.{tc}.{q8}")
                        vguard(ve)
                        i1 = ve.tensor_copy(resb[:, sp][:, q8 * 128:(q8 + 1) * 128],
                                            ptr3[q8 % 2][:, 0:128])
                        inc(i1, sVEa, 1, mark=f"p3.rtev.{ch}.{tc}.{q8}")
                    w(ve, sPEb, f"p3.mm.{ch}.{tc}")
                    if tc >= 2 or ch > 0:
                        prev = (f"p3.ost.{ch}.{tc - 2}" if tc >= 2 else
                                f"p3.ost.{ch - 1}.{6 + tc}")
                        w(ve, sOST[tc % 2], prev)
                    w(ve, sRX[tc % 2], f"p3.rx.{ch}.{tc}")
                    vguard(ve)
                    if dbg and ch == 1 and tc == 0:
                        inc(ve.tensor_copy(attcp[:], pout[sp][:, :]), sVEa, 1,
                            mark="dbg.attcp")
                    ve.tensor_tensor(outb[:, sp], pout[sp][:, :], resb[:, sp],
                                     ALU.add)
                    i1 = ve.tensor_add(outb[:, sp], outb[:, sp], xhb[:, sp])
                    inc(i1, sVEa, 1, mark=f"p3.oe.{ch}.{tc}")

        P.run_phase({"sync": p3_sync, "gpsimd": p3_gp, "tensor": p3_te,
                     "vector": p3_ve})

    top.close()
    return nc


_NC_CACHE = {}


def kernel(L0_attn_x, L0_attn_W, L0_attn_g1, L0_attn_g2, L0_attn_g3,
           L0_ffn_weight, L0_ffn_bias):
    x = np.ascontiguousarray(np.asarray(L0_attn_x, dtype=np.float32))
    W = np.ascontiguousarray(np.asarray(L0_attn_W, dtype=np.float32))
    g1 = np.asarray(L0_attn_g1, dtype=np.float32)
    g2 = np.asarray(L0_attn_g2, dtype=np.float32)
    g3 = np.asarray(L0_attn_g3, dtype=np.float32)
    wgt = np.ascontiguousarray(np.asarray(L0_ffn_weight, dtype=np.float32))
    bias = np.asarray(L0_ffn_bias, dtype=np.float32)

    B = x.shape[0]
    WTc = np.ascontiguousarray(W.T)
    wgtTc = np.ascontiguousarray(wgt.T)
    biastc = np.ascontiguousarray(bias.reshape(16, 128).T)
    g1c = np.ascontiguousarray(g1.reshape(128, 1))
    g2c = np.ascontiguousarray(g2.reshape(128, 1))
    g3bcc = np.ascontiguousarray(np.broadcast_to(g3.reshape(1, 128), (128, 128)))
    identc = np.eye(128, dtype=np.float32)
    selc = np.zeros((16, 2048), dtype=np.float32)
    for k in range(16):
        selc[k, k * 128:(k + 1) * 128] = 1.0
    aonesc = np.ones((128, 128), dtype=np.float32)
    negcbfc = np.full((128, 1), -1.0, dtype=ml_dtypes.bfloat16)
    onescc = np.ones((128, 1), dtype=np.float32)
    cshiftc = np.full((128, 1), C_SHIFT, dtype=np.float32)

    in_maps = []
    for c in range(8):
        b = c // 2
        half = c % 2
        mine = slice(half * TH, (half + 1) * TH)
        other = slice((1 - half) * TH, (2 - half) * TH)
        xp = np.concatenate([x[b][mine], x[b][other]], axis=0)
        in_maps.append({
            "xT": np.ascontiguousarray(xp.T),
            "xhd": np.ascontiguousarray(x[b][mine]),
            "WTm": WTc, "Wn": W, "wgtT": wgtTc, "biastd": biastc,
            "g1d": g1c, "g2d": g2c, "g3bcd": g3bcc, "identd": identc,
            "seld": selc, "aonesd": aonesc, "negcbfd": negcbfc,
            "onescd": onescc, "cshiftd": cshiftc,
        })

    if "nc" not in _NC_CACHE:
        _NC_CACHE["nc"] = build_nc()
    nc = _NC_CACHE["nc"]
    r = run_bass_kernel_spmd(nc, in_maps, core_ids=list(range(8)))
    outp = np.empty((B, T, D), dtype=np.float32)
    for c in range(8):
        b = c // 2
        half = c % 2
        outp[b, half * TH:(half + 1) * TH, :] = r.results[c]["out"]
    return outp
